# version 41
# baseline (speedup 1.0000x reference)
import sys
if '/opt/trn_rl_repo' not in sys.path:
    sys.path.insert(0, '/opt/trn_rl_repo')

import numpy as np
import ml_dtypes
import concourse.bass as bass
import concourse.mybir as mybir
from concourse.bass_utils import run_bass_kernel_spmd
from contextlib import ExitStack

F32 = mybir.dt.float32
BF16 = mybir.dt.bfloat16
AF = mybir.ActivationFunctionType
ALU = mybir.AluOpType
AXX = mybir.AxisListType.X
BF = ml_dtypes.bfloat16

T = 2048
D = 1024
H = 1024
NG = 8
NH = 8
DH = 64
INNER = 512
HID = 2730
W1N = 2 * HID
WIN = 512
NL = 2
MASKV = -float(2.0 ** 100)
NHCH = 22
HCH = [(j * 128, min(128, HID - j * 128)) for j in range(NHCH)]
W1PN = 2 * 1024 * HID
W2PN = 8 * (128 * 21 * 128 + 42 * 128)
LAYER_PN = W1PN + W2PN


def _host_consts():
    c = {}
    c["c_ident32"] = np.eye(128, dtype=np.float32)
    c["c_identb"] = np.eye(128).astype(BF)
    st = np.zeros((128, 128), np.float32)
    tr = np.zeros((128, 128), np.float32)
    for i in range(128):
        st[i, :i] = MASKV
        tr[i, i + 1:] = MASKV
    c["c_stair"] = st
    c["c_triag"] = tr
    c["c_stairb"] = st.astype(BF)
    c["c_triagb"] = tr.astype(BF)
    sq = np.zeros((128, 32), np.float64)
    for g in range(4):
        for p in range(128):
            sq[p, g * 8 + 2 * g + p // 64] = 1.0
    c["c_selsq"] = sq.astype(BF)
    sb_ = np.zeros((8, 4 * 128), np.float32)
    for g in range(4):
        for p in range(128):
            sb_[2 * g + p // 64, g * 128 + p] = 1.0
    c["c_selB"] = sb_
    sb2 = np.zeros((40, 4 * 128), np.float32)
    sb2[32:40] = sb_
    c["c_selB2"] = sb2
    c["c_onesb"] = np.ones((128, 1)).astype(BF)
    jm = np.zeros((128, 128), np.float32)
    for k_ in range(128):
        m_ = (k_ // 64) * 64 + ((k_ % 64) + 32) % 64
        jm[k_, m_] = 1.0
    c["c_jmat"] = jm
    c["c_jmatb"] = jm.astype(BF)
    sc_vals = [1e-5, -0.5, 1.0 / 1024.0, -1.0 / 1024.0, -1.0, 8.0, 0.5, 1e-30]
    c["c_scal"] = np.tile(np.array(sc_vals, np.float32)[None, :], (128, 1))
    c["c_onesr"] = np.ones((1, 128), np.float32)
    j = np.arange(0, DH, 2, dtype=np.float64)
    inv = 1.0 / (10000.0 ** (j / DH))
    t = np.arange(T, dtype=np.float64)
    fr = t[:, None] * inv[None, :]
    base = (j + 0.4 * DH) / (1.4 * DH)
    power = (t - T // 2) / (WIN // 2)
    scale = base[None, :] ** power[:, None]
    cos, sin = np.cos(fr), np.sin(fr)

    def expand(m):
        return np.concatenate([m, m], axis=1).T
    sgn = np.where(np.arange(64) < 32, -1.0, 1.0)[:, None]
    tabs = np.stack([expand(cos * scale), expand(sin * scale) * sgn,
                     expand(cos / scale), expand(sin / scale) * sgn])
    tabs = np.concatenate([tabs, tabs], axis=1)
    hi = tabs.astype(np.float32)
    lo = (tabs - hi.astype(np.float64)).astype(np.float32)
    c["c_tabhi"] = hi
    c["c_tablo"] = lo
    return c


class Sem:
    __slots__ = ("h", "n")

    def __init__(self, stack, nc, name):
        self.h = stack.enter_context(nc.semaphore(name))
        self.n = 0


class KB:
    def __init__(self, debug=False):
        self.debug = debug
        nc = bass.Bass("TRN2", target_bir_lowering=False)
        self.nc = nc
        st = ExitStack()
        self.stack = st

        dt = nc.dram_tensor
        self.d_x = dt("x", [T, D], F32, kind="ExternalInput").ap()
        self.d_a = dt("ssm_a", [D], F32, kind="ExternalInput").ap()
        self.d_b = dt("ssm_b", [D], F32, kind="ExternalInput").ap()
        self.d_woutp = dt("woutp", [D * D], F32, kind="ExternalInput").ap()
        self.d_bout = dt("b_out", [D], F32, kind="ExternalInput").ap()
        self.d_wqkv = dt("wqkv", [NL, D, 3 * INNER], F32, kind="ExternalInput").ap()
        self.d_qs = dt("q_scale", [NL, DH], F32, kind="ExternalInput").ap()
        self.d_ks = dt("k_scale", [NL, DH], F32, kind="ExternalInput").ap()
        self.d_wo = dt("wo", [NL, INNER, D], F32, kind="ExternalInput").ap()
        self.d_lng = dt("ln_g", [NL, D], F32, kind="ExternalInput").ap()
        self.d_lnb = dt("ln_b", [NL, D], F32, kind="ExternalInput").ap()
        self.d_w1p = dt("w1p", [NL * W1PN], F32, kind="ExternalInput").ap()
        self.d_b1 = dt("b1", [NL, W1N], F32, kind="ExternalInput").ap()
        self.d_w2p = dt("w2p", [NL * W2PN], F32, kind="ExternalInput").ap()
        self.d_b2 = dt("b2", [NL, D], F32, kind="ExternalInput").ap()
        self.consts = _host_consts()
        for k, v in self.consts.items():
            setattr(self, "d" + k, dt(k, list(v.shape),
                                      BF16 if v.dtype == BF else F32,
                                      kind="ExternalInput").ap())
        self.d_out = dt("out", [T, D], F32, kind="ExternalOutput").ap()
        self.d_park = dt("park", [NG * 128, T], F32, kind="Internal").ap()
        self.pre_off = {}
        off = 0
        for l in range(NL):
            for j in range(NHCH):
                _, ln = HCH[j]
                self.pre_off[(l, "w1", j)] = off
                off += 2 * 1024 * ln
            for m in range(8):
                self.pre_off[(l, "w2", m)] = off
                off += 128 * 21 * 128
                self.pre_off[(l, "w2t", m)] = off
                off += 42 * 128
        self.pre_off["wout"] = off
        off += D * D
        self.d_wpre = dt("wpre", [off], BF16, kind="Internal").ap()
        if debug:
            self.d_dbg = {}
            for nm, shape in [("dbg_xn", [NG * 128, T]),
                              ("dbg_q", [4 * 128, T]), ("dbg_k", [4 * 128, T]),
                              ("dbg_v", [16 * 128, INNER]),
                              ("dbg_o", [4 * 128, T]),
                              ("dbg_u1", [NG * 128, T]),
                              ("dbg_qh", [4 * 128, H]),
                              ("dbg_rot", [4 * 128, H]),
                              ("dbg_rq", [8, H])]:
                self.d_dbg[nm] = dt(nm, shape, F32, kind="ExternalOutput").ap()

        def sb(name, shape, dty):
            return st.enter_context(nc.sbuf_tensor(name, shape, dty))
        self.u = [sb(f"u{g}", [128, H + 1], F32) for g in range(NG)]
        self.parkSB = [sb(f"parkSB{g}", [128, H], BF16) for g in range(NG)]
        self.stg = [sb(f"stg{g}", [128, H], BF16) for g in range(NG)]
        self.Bb = [sb(f"Bb{i}", [128, H], F32) for i in range(NG)]
        self.tabsb = sb("tabsb", [128, 4 * H], BF16)
        self.UN = sb("UN", [128, 6400], F32)
        self.WS = sb("WS", [128, 4096], F32)
        self.probs = sb("probs", [128, NH * 640], BF16)
        self.attnT = [sb(f"attnT{i}", [128, 640], BF16) for i in range(2)]
        self.halo_k = [sb(f"halok{l}", [128, 4 * 512], BF16) for l in range(NL)]
        self.halo_v = [sb(f"halov{l}", [128, 4 * 512], BF16) for l in range(NL)]
        self.rowsSB = sb("rowsSB", [1, 2048], F32)
        self.rq = sb("rq", [8, H], F32)
        self.rk40 = sb("rk40", [40, H], F32)
        self.acol = sb("acol", [128, 8], F32)
        self.ah8 = sb("ah8", [128, 8], F32)
        self.bcol = sb("bcol", [128, 8], F32)
        self.boutc = sb("boutc", [128, 8], F32)
        self.qs8 = sb("qs8", [128, 1], F32)
        self.ks8 = sb("ks8", [128, 1], F32)
        self.lgc = sb("lgc", [128, 8], F32)
        self.lbc = sb("lbc", [128, 8], F32)
        self.b1a = sb("b1a", [128, 22], F32)
        self.b1g = sb("b1g", [128, 22], F32)
        self.b2c = sb("b2c", [128, 8], F32)
        self.nmax = sb("nmax", [128, 8], F32)
        self.rsum = sb("rsum", [128, 8], F32)
        self.rcp = sb("rcp", [128, 8], F32)
        self.carry = sb("carry", [128, 8], F32)
        self.hQ = [sb(f"hQ{l}", [128, 8], F32) for l in range(NL)]
        self.hL = [sb(f"hL{l}", [128, 8], F32) for l in range(NL)]
        self.ident32 = sb("ident32", [128, 128], F32)
        self.identb = sb("identb", [128, 128], BF16)
        self.stair = sb("stair", [128, 128], F32)
        self.triag = sb("triag", [128, 128], F32)
        self.stairb = sb("stairb", [128, 128], BF16)
        self.triagb = sb("triagb", [128, 128], BF16)
        self.selsq = sb("selsq", [128, 32], BF16)
        self.selB = sb("selB", [8, 4 * 128], F32)
        self.selB2 = sb("selB2", [40, 4 * 128], F32)
        self.onesb = sb("onesb", [128, 1], BF16)
        self.onesr = sb("onesr", [1, 128], F32)
        self.jmat = sb("jmat", [128, 128], F32)
        self.jmatb = sb("jmatb", [128, 128], BF16)
        self.scal = sb("scal", [128, 8], F32)
        for i, v in enumerate([1e-5, -0.5, 1.0 / 1024.0, -1.0 / 1024.0,
                               -1.0, 8.0, 0.5, 1e-30]):
            nc.const_aps.register(F32, v, self.scal[:, i:i + 1])

        self.psA = st.enter_context(nc.psum_tensor("psA", [128, 2048], F32))
        self.psB = st.enter_context(nc.psum_tensor("psB", [128, 2048], F32))

        self.sW = Sem(st, nc, "sW")
        self.sX = Sem(st, nc, "sX")
        self.sXG = Sem(st, nc, "sXG")
        self.sXp = [Sem(st, nc, "sXp0"), Sem(st, nc, "sXp1")]
        self.sWp = [Sem(st, nc, "sWp0"), Sem(st, nc, "sWp1")]
        self.sDp = [Sem(st, nc, "sDp0"), Sem(st, nc, "sDp1")]
        self.sPC = [Sem(st, nc, "sPC0"), Sem(st, nc, "sPC1")]
        self.sFu = [Sem(st, nc, f"sFu{u}") for u in range(4)]
        self.sPW = Sem(st, nc, "sPW")
        self.sPp = [Sem(st, nc, "sPp0"), Sem(st, nc, "sPp1")]
        self.sT = Sem(st, nc, "sT")
        self.sV = Sem(st, nc, "sV")
        self.sS = Sem(st, nc, "sS")
        self.sG = Sem(st, nc, "sG")
        self.sP = Sem(st, nc, "sP")
        self._waited = {}
        self._auto_evs = {}
        self._auto_refs = []
        self.V = EngProxy(self, nc.vector, "vector", self.sV)
        self.S = EngProxy(self, nc.scalar, "scalar", self.sS)
        self.G = EngProxy(self, nc.gpsimd, "gpsimd", self.sG)
        self.Vn = EngProxy(self, nc.vector, "vector", self.sV, nosync=True)
        self.Sn = EngProxy(self, nc.scalar, "scalar", self.sS, nosync=True)

    def ev(self, sem, instr, k=1):
        ae = self._auto_evs.get(id(instr))
        if ae is not None:
            assert ae[0] is sem, "event sem mismatch with auto-sync sem"
            return ae
        instr.then_inc(sem.h, k)
        sem.n += k
        return (sem, sem.n)

    def mark(self, sem):
        return (sem, sem.n)

    def w(self, eng, engname, *events, future=False):
        d = self._waited.setdefault(engname, {})
        need = {}
        for e in events:
            if e is None:
                continue
            sem, n = e
            assert future or n <= sem.n, \
                f"{engname} waits future event ({n} > {sem.n})"
            if n > 0:
                need[sem] = max(need.get(sem, 0), n)
        for sem, n in need.items():
            if d.get(sem, -1) >= n:
                continue
            eng.wait_ge(sem.h, n)
            d[sem] = n

    def wf(self, eng, engname, *events):
        self.w(eng, engname, *events, future=True)

    def bview(self, i):
        return self.Bb[i][:, 0:H].bitcast(BF16)

    def qp(self, g):
        return self.bview(g)[:, 0:H]

    def kp(self, g):
        return self.bview(g)[:, H:2 * H]

    def vtm(self, tb):
        return self.bview(4 + tb // 4)[:, (tb % 4) * 512:(tb % 4 + 1) * 512]

    def odm(self, g):
        return self.bview(6 + g // 2)[:, (g % 2) * H:(g % 2 + 1) * H]

    def un(self, off, n):
        return self.UN[:, off:off + n]

    def unb(self, off_f32, n_bf16):
        return self.UN[:, off_f32:off_f32 + (n_bf16 + 1) // 2].bitcast(BF16)[:, 0:n_bf16]

    def wsb(self, par):
        return self.WS[:, par * 2048:(par + 1) * 2048].bitcast(BF16)

    def wsu(self, u):
        return self.WS[:, u * 1024:(u + 1) * 1024].bitcast(BF16)

    def wsu2(self, u0):
        return self.WS[:, u0 * 1024:(u0 + 2) * 1024].bitcast(BF16)

    def psb(self, bank, cols=512):
        ps = self.psA if bank < 4 else self.psB
        c0 = (bank % 4) * 512
        return ps[:, c0:c0 + cols]



_DATA_OPS = {
    "tensor_copy", "tensor_tensor", "tensor_add", "tensor_sub", "tensor_mul",
    "tensor_max", "tensor_scalar", "tensor_scalar_add", "tensor_scalar_sub",
    "tensor_scalar_mul", "tensor_scalar_max", "tensor_scalar_min",
    "tensor_single_scalar", "tensor_reduce", "reduce_max", "reduce_sum",
    "scalar_tensor_tensor", "tensor_tensor_scan", "memset", "memzero",
    "reciprocal", "select", "stream_shuffle", "transpose", "activation",
    "copy", "mul", "add", "square", "sqrt", "sign", "iota", "tensor_relu",
}


class EngProxy:

    def __init__(self, B, eng, name, sem, nosync=False):
        self._B = B
        self._eng = eng
        self._name = name
        self._sem = sem
        self._nosync = nosync

    def __getattr__(self, k):
        f = getattr(self._eng, k)
        if k not in _DATA_OPS:
            return f
        B, eng, sem, name = self._B, self._eng, self._sem, self._name
        nosync = self._nosync

        def wrap(*a, **kw):
            ins = f(*a, **kw)
            ins.then_inc(sem.h, 1)
            sem.n += 1
            if not nosync:
                eng.wait_ge(sem.h, sem.n)
                B._waited.setdefault(name, {})[sem] = sem.n
            B._auto_evs[id(ins)] = (sem, sem.n)
            B._auto_refs.append(ins)
            return ins
        return wrap


class _SubBlock:

    def __init__(self, blk):
        self.blk = blk

    def __enter__(self):
        return self.blk

    def __exit__(self, *a):
        return False


def _maybe_block(nc, blk):
    return nc.Block() if blk is None else _SubBlock(blk)


def dram3(ap_like, offset, dims):
    return bass.AP(ap_like.tensor, offset, dims)


class WStream:

    def __init__(self, B):
        self.B = B
        self.idx = 0
        self.done_ev = {}
        self.load_ev = {}

    def issue(self, g, dmas, engname="gpsimd", sems=None):
        B = self.B
        prev = self.idx - 2
        if prev >= 0:
            B.w(g, engname, self.done_ev[prev], future=True)
        sem = (sems or B.sWp)[self.idx % 2]
        for dst, src in dmas:
            B.ev(sem, g.dma_start(out=dst, in_=src), 16)
        self.load_ev[self.idx] = B.mark(sem)
        gi = self.idx
        self.idx += 1
        return gi, gi % 2

    def done(self, gi, event):
        self.done_ev.setdefault(gi, event)


def build(debug=False):
    B = KB(debug)
    B.ws = WStream(B)
    B.final_done = None
    B.park_done = None
    phase_setup(B)
    for h in range(2):
        if debug:
            phase_p0(B, h)
            phase_scan(B, h)
            for l in range(NL):
                phase_qkv(B, h, l)
                if l == 0:
                    phase_dbg_qkv(B, h)
                phase_attn(B, h, l)
                if l == 0:
                    phase_dbg_o(B, h)
                phase_wo(B, h, l)
                phase_ln(B, h, l)
                phase_ffn(B, h, l)
            phase_dbg_u(B, h, "dbg_u1")
            phase_final(B, h)
        else:
            with B.nc.Block() as b1:
                phase_p0(B, h, b1)
                phase_scan(B, h, b1)
                for l in range(NL):
                    phase_qkv(B, h, l, b1)
                    phase_attn(B, h, l, b1)
                    phase_wo(B, h, l, b1)
                    phase_ln(B, h, l, b1)
                    phase_ffn(B, h, l, b1)
                phase_final(B, h, b1)

    with B.nc.Block() as blk:
        @blk.sync
        def _(s):
            B.w(s, "sync", B.mark(B.sP), B.mark(B.sPp[0]), B.mark(B.sPp[1]))
    B.stack.close()
    return B


def phase_setup(B):
    with B.nc.Block() as blk:
        @blk.sync
        def _(s):
            for nm, tile in [("c_ident32", B.ident32), ("c_identb", B.identb),
                             ("c_stair", B.stair), ("c_triag", B.triag),
                             ("c_selsq", B.selsq), ("c_onesb", B.onesb),
                             ("c_stairb", B.stairb), ("c_triagb", B.triagb),
                             ("c_onesr", B.onesr), ("c_selB", B.selB),
                             ("c_jmat", B.jmat), ("c_jmatb", B.jmatb),
                             ("c_selB2", B.selB2), ("c_scal", B.scal)]:
                B.ev(B.sX, s.dma_start(out=tile[:], in_=getattr(B, "d" + nm)), 16)
            for dsrc, dst in [(B.d_a, B.acol), (B.d_b, B.bcol),
                              (B.d_bout, B.boutc)]:
                B.ev(B.sX, s.dma_start(
                    out=dst[:], in_=dram3(dsrc, 0, [[1, 128], [128, 8]]),
                    allow_slow_non_contiguous=True), 16)
        ld = B.mark(B.sX)

        @blk.scalar
        def _(sc):

            sc = B.S
            B.w(sc, "scalar", ld)
            B.ev(B.sS, sc.activation(B.ah8[:], B.acol[:], AF.Tanh))

        @blk.vector
        def _(v):

            v = B.V
            B.ev(B.sV, v.memset(B.carry[:], 0.0))


def emit_precast(B, g, l, chunks):
    CH = 43680
    for c in chunks:
        if c < 4:
            n = W1PN // 4
            src_t, src_o = B.d_w1p.tensor, l * W1PN + c * n
            dst_o = l * LAYER_PN + c * n
        else:
            n = W2PN // 2
            src_t, src_o = B.d_w2p.tensor, l * W2PN + (c - 4) * n
            dst_o = l * LAYER_PN + W1PN + (c - 4) * n
        B.ev(B.sPC[l], g.dma_start(
            out=bass.AP(B.d_wpre.tensor, dst_o, [[CH, n // CH], [1, CH]]),
            in_=bass.AP(src_t, src_o, [[CH, n // CH], [1, CH]])), 16)


def phase_p0(B, h, blk_=None):
    nc = B.nc
    t0 = h * H
    XT = [0, 1024]
    XSQ = 2048
    XN = [3072, None]
    C = 4200

    def stats(par):
        c = C + 8 * par
        return (B.un(c + 0, 1), B.un(c + 1, 1), B.un(c + 2, 1),
                B.un(c + 4, 1), B.un(c + 5, 1), B.un(c + 6, 1),
                B.un(c + 7, 1))
    TLO = 4300

    with _maybe_block(nc, blk_) as blk:
        x_ev = [None] * 8
        tp_done = [None] * 8
        u_evac = [None] * 8
        sums_ev = [None] * 8
        sb = B.sS.n
        xn_done = [(B.sS, sb + 4 * (tb + 1)) for tb in range(8)]

        def xnview(tb):
            if tb % 2 == 0:
                return B.un(XN[0], 1024)
            return B.probs[:, 0:2048].bitcast(F32)[:, 0:1024]

        def issue_x(s, tb):
            sem = B.sXp[tb % 2]
            B.ev(sem, s.dma_start(
                out=B.un(XT[tb % 2], 1024),
                in_=B.d_x[t0 + tb * 128:t0 + (tb + 1) * 128, :]), 16)
            x_ev[tb] = B.mark(sem)

        @blk.sync
        def _(s):
            if B.final_done is not None:
                B.w(s, "sync", *B.final_done)
            issue_x(s, 0)
            issue_x(s, 1)

        if h == 0 and B.sPW.n == 0:
            @blk.gpsimd
            def _(g):
                for c in range(2):
                    n = D * D // 2
                    B.ev(B.sPW, g.dma_start(
                        out=bass.AP(B.d_wpre.tensor, B.pre_off["wout"] + c * n,
                                    [[65536, n // 65536], [1, 65536]]),
                        in_=bass.AP(B.d_woutp.tensor, c * n,
                                    [[65536, n // 65536], [1, 65536]])), 16)

        def emit_stats(tb):
            @blk.vector
            def _(v, tb=tb):
                v = B.V
                if B.final_done is not None:
                    B.w(v, "vector", *B.final_done)
                B.w(v, "vector", x_ev[tb])
                if tb >= 2:
                    B.wf(v, "vector", xn_done[tb - 2])
                m_neg, msq_n, var, r_col, bias_c, s1, s2 = stats(tb % 2)
                v.tensor_mul(B.un(XSQ, 1024), B.un(XT[tb % 2], 1024),
                             B.un(XT[tb % 2], 1024))
                v.reduce_sum(s1, B.un(XT[tb % 2], 1024), axis=AXX)
                v.reduce_sum(s2, B.un(XSQ, 1024), axis=AXX)
                v.tensor_scalar_mul(m_neg, s1, -1.0 / D)
                v.scalar_tensor_tensor(msq_n, m_neg, -1.0, m_neg,
                                       ALU.mult, ALU.mult)
                sums_ev[tb] = B.ev(B.sV, v.scalar_tensor_tensor(
                    var, s2, 1.0 / D, msq_n, ALU.mult, ALU.add))

        def emit_tp(tb):
            @blk.tensor
            def _(t, tb=tb):
                B.wf(t, "tensor", xn_done[tb])
                if tb >= 2:
                    B.w(t, "tensor", u_evac[tb - 2])
                reg = B.psb(0 if tb % 2 == 0 else 4, 1024)
                xv = xnview(tb)
                for g in range(NG):
                    ins = t.transpose(reg[:, g * 128:(g + 1) * 128],
                                      xv[:, g * 128:(g + 1) * 128],
                                      B.ident32[:])
                tp_done[tb] = B.ev(B.sT, ins)

            @blk.vector
            def _(v, tb=tb):
                v = B.V
                B.w(v, "vector", tp_done[tb])
                reg = B.psb(0 if tb % 2 == 0 else 4, 1024)
                for g in range(NG):
                    vv = B.Vn if g < NG - 1 else B.V
                    ins = vv.tensor_copy(
                        B.u[g][:, 1 + tb * 128:1 + (tb + 1) * 128],
                        reg[:, g * 128:(g + 1) * 128])
                u_evac[tb] = B.ev(B.sV, ins)

        for tb in range(8):
            emit_stats(tb)
            if tb + 2 < 8:
                @blk.sync
                def _(s, tb=tb):
                    B.wf(s, "sync", xn_done[tb])
                    issue_x(s, tb + 2)
            if tb >= 1:
                emit_tp(tb - 1)
        emit_tp(7)

        @blk.scalar
        def _(sc):
            sc = B.S
            for tb in range(8):
                m_neg, msq_n, var, r_col, bias_c, s1, s2 = stats(tb % 2)
                B.w(sc, "scalar", sums_ev[tb])
                if tb >= 2:
                    B.w(sc, "scalar", tp_done[tb - 2])
                sc.activation(var, var, AF.Ln, bias=1e-5)
                sc.activation(r_col, var, AF.Exp, scale=-0.5)
                sc.activation(bias_c, m_neg, AF.Identity,
                              scale=r_col[:, 0:1])
                ev = B.ev(B.sS, sc.activation(
                    xnview(tb), B.un(XT[tb % 2], 1024), AF.Identity,
                    bias=bias_c[:, 0:1], scale=r_col[:, 0:1]))
                assert ev == xn_done[tb], (ev, xn_done[tb])

        B.tabadd_ev = [None] * 4

        @blk.sync
        def _(s):
            if B.final_done is not None:
                B.w(s, "sync", *B.final_done)
            for k in range(4):
                B.ev(B.sX, s.dma_start(
                    out=B.Bb[2 * k][:, 0:H],
                    in_=bass.AP(B.dc_tabhi.tensor, k * 128 * T + t0,
                                [[T, 128], [1, H]])), 16)
                B.ev(B.sX, s.dma_start(
                    out=B.Bb[2 * k + 1][:, 0:H],
                    in_=bass.AP(B.dc_tablo.tensor, k * 128 * T + t0,
                                [[T, 128], [1, H]])), 16)
            B.tabld_ev = B.mark(B.sX)

        @blk.vector
        def _(v):

            v = B.V
            B.w(v, "vector", B.tabld_ev)
            for k in range(4):
                v.tensor_add(B.Bb[2 * k][:, 0:H], B.Bb[2 * k][:, 0:H],
                             B.Bb[2 * k + 1][:, 0:H])
                B.tabadd_ev[k] = B.ev(B.sV, v.tensor_copy(
                    B.tabsb[:, k * H:(k + 1) * H], B.Bb[2 * k][:, 0:H]))

        @blk.vector
        def _(v):

            v = B.V
            for g in range(NG):
                ins = v.memset(B.u[g][:, 0:1], 0.0)
            B.ev(B.sV, ins)


def phase_scan(B, h, blk_=None):
    nc = B.nc
    t0 = h * H
    YB = 0
    WOL = [4096, 4608]
    PKS = 5120

    with _maybe_block(nc, blk_) as blk:
        ycast_ev = [None] * NG
        for g in range(NG):
            @blk.vector
            def _(v):

                v = B.V
                e = B.ev(B.sV, v.tensor_scalar_mul(
                    B.un(PKS, H), B.u[g][:, 1:H + 1], B.bcol[:, g:g + 1]))
                B.w(v, "vector", e)
                init = 0.0 if h == 0 else B.carry[:, g:g + 1]
                d0 = B.ah8[:, g:g + 1].broadcast_to([128, H])
                e = B.ev(B.sV, v.tensor_tensor_scan(
                    B.Bb[g][:, 0:H], d0, B.un(PKS, H), init, ALU.mult, ALU.add))
                if h == 0:
                    B.w(v, "vector", e)
                    v.tensor_copy(B.carry[:, g:g + 1], B.Bb[g][:, H - 1:H])
                ycast_ev[g] = B.ev(B.sV, v.tensor_copy(
                    B.unb(YB + g * 512, H), B.Bb[g][:, 0:H]))

        ev_dma = [None] * 8
        ev_mm = [None] * 8
        ev_evac = [None] * 8

        def issue_wout(s, ec):
            if ec >= 2:
                B.w(s, "sync", ev_mm[ec - 2])
            elif h == 0:
                B.w(s, "sync", (B.sPW, B.sPW.n))
            sem = B.sDp[ec % 2]
            B.ev(sem, s.dma_start(
                out=B.wsb(ec % 2)[:, 0:1024],
                in_=bass.AP(B.d_wpre.tensor, B.pre_off["wout"] + ec * 1024 * 128,
                            [[1024, 128], [1, 1024]])), 16)
            ev_dma[ec] = B.mark(sem)

        @blk.sync
        def _(s):
            issue_wout(s, 0)
            issue_wout(s, 1)

        for ec in range(8):
            @blk.tensor
            def _(t):
                B.w(t, "tensor", ev_dma[ec], *ycast_ev)
                if ec >= 2:
                    B.w(t, "tensor", ev_evac[ec - 2])
                reg = B.psb(0 if ec % 2 == 0 else 2, 1024)
                for kt in range(NG):
                    for tc in range(2):
                        ins = t.matmul(
                            reg[:, tc * 512:(tc + 1) * 512],
                            B.wsb(ec % 2)[:, kt * 128:(kt + 1) * 128],
                            B.unb(YB + kt * 512, H)[:, tc * 512:(tc + 1) * 512],
                            start=(kt == 0), stop=(kt == NG - 1))
                ev_mm[ec] = B.ev(B.sT, ins)

            @blk.scalar
            def _(sc):

                sc = B.S
                B.w(sc, "scalar", ev_mm[ec])
                reg = B.psb(0 if ec % 2 == 0 else 2, 1024)
                ev_evac[ec] = B.ev(B.sS, sc.activation(
                    B.parkSB[ec][:], reg, AF.Identity,
                    bias=B.boutc[:, ec:ec + 1]))

            B.scan_evac = ev_evac
            B.scan_mm = ev_mm

            if ec + 2 < 8:
                @blk.sync
                def _(s):
                    issue_wout(s, ec + 2)


def _load_cols_layer(B, s, l):
    B.ev(B.sX, s.dma_start(out=B.qs8[0:64, 0:1],
                           in_=dram3(B.d_qs, l * DH, [[1, 64], [1, 1]])), 16)
    B.ev(B.sX, s.dma_start(out=B.qs8[64:128, 0:1],
                           in_=dram3(B.d_qs, l * DH, [[1, 64], [1, 1]])), 16)
    B.ev(B.sX, s.dma_start(out=B.ks8[0:64, 0:1],
                           in_=dram3(B.d_ks, l * DH, [[1, 64], [1, 1]])), 16)
    B.ev(B.sX, s.dma_start(out=B.ks8[64:128, 0:1],
                           in_=dram3(B.d_ks, l * DH, [[1, 64], [1, 1]])), 16)
    B.ev(B.sX, s.dma_start(out=B.lgc[:],
                           in_=dram3(B.d_lng, l * D, [[1, 128], [128, 8]]),
                           allow_slow_non_contiguous=True), 16)
    B.ev(B.sX, s.dma_start(out=B.lbc[:],
                           in_=dram3(B.d_lnb, l * D, [[1, 128], [128, 8]]),
                           allow_slow_non_contiguous=True), 16)
    B.ev(B.sX, s.dma_start(out=B.b2c[:],
                           in_=dram3(B.d_b2, l * D, [[1, 128], [128, 8]]),
                           allow_slow_non_contiguous=True), 16)
    B.ev(B.sX, s.dma_start(out=B.b1a[:, 0:21],
                           in_=dram3(B.d_b1, l * W1N, [[1, 128], [128, 21]]),
                           allow_slow_non_contiguous=True), 16)
    B.ev(B.sX, s.dma_start(out=B.b1a[0:42, 21:22],
                           in_=dram3(B.d_b1, l * W1N + 21 * 128,
                                     [[1, 42], [1, 1]])), 16)
    B.ev(B.sX, s.dma_start(out=B.b1g[:, 0:21],
                           in_=dram3(B.d_b1, l * W1N + HID, [[1, 128], [128, 21]]),
                           allow_slow_non_contiguous=True), 16)
    B.ev(B.sX, s.dma_start(out=B.b1g[0:42, 21:22],
                           in_=dram3(B.d_b1, l * W1N + HID + 21 * 128,
                                     [[1, 42], [1, 1]])), 16)
    return B.mark(B.sX)


def phase_qkv(B, h, l, blk_=None):
    nc = B.nc
    SQB = [4096, 4608]
    ROTB, QHB = 5120, 5632

    def reg_of(slot):
        return [B.psA[:, 0:1024], B.psA[:, 1024:2048], B.psB[:, 0:1024]][slot]

    def rawb(c):
        return B.unb((c % 4) * 512 + (0 if c < 4 else 2048), 1024)

    with _maybe_block(nc, blk_) as blk:
        mm_ev = [None] * 12
        cons_ev = [None] * 12
        qraw_ev = [None] * 12
        n2done = [None, None]
        rows_ev = [None, None]
        bc_ev = [None] * 8
        apply_ev = [None] * 8
        rotmm_ev = [None] * 8
        rotev_ev = [None] * 8
        t2_ev = [None] * 8
        B.qprime_ev = [None] * 4
        B.kprime_ev = [None] * 4
        vtp_ev = [None] * 4
        B.vev_ev = [None] * 4
        grp_of = {}
        ld_of = {}

        sb = B.sS.n
        B.qs8_ev = (B.sS, sb + 1)
        qevac_pred = [None] * 8
        for c in range(4):
            qevac_pred[c] = (B.sS, sb + 2 + c)
            qevac_pred[4 + c] = (B.sS, sb + 8 + c)
        sqrt_pred = [(B.sS, sb + 7), (B.sS, sb + 13)]
        vevac_pred = [(B.sS, sb + 14 + c) for c in range(4)]
        rot_pred = [(B.sS, sb + 18 + 9 * g_) for g_ in range(4)] + \
            [(B.sS, sb + 54 + i) for i in range(4)]
        vev_pred = [(B.sS, sb + 26 + 9 * g_) for g_ in range(4)]
        B.vev_ev = vev_pred
        B.vevac_pred = vevac_pred
        B.rot_pred = rot_pred

        @blk.sync
        def _(s):
            if l == 1:
                B.w(s, "sync", *B.ffn_done_evs)
            B.lcols_ev = _load_cols_layer(B, s, l)

        @blk.vector
        def _(v):
            v = B.V
            if h == 0:
                for g in range(NG):
                    vv = B.Vn if g < NG - 1 else B.V
                    ins = vv.tensor_copy(B.hQ[l][:, g:g + 1],
                                         B.u[g][:, H:H + 1])
                B.ev(B.sV, ins)
            else:
                for g in range(NG):
                    vv = B.Vn if g < NG - 1 else B.V
                    ins = vv.tensor_copy(B.u[g][:, 0:1], B.hQ[l][:, g:g + 1])
                B.ev(B.sV, ins)
            stg_ev = []
            for g in range(NG):
                off = 1 if g < 4 else 0
                vv = B.Vn if g < NG - 1 else B.V
                stg_ev.append(B.ev(B.sV, vv.tensor_copy(
                    B.stg[g][:], B.u[g][:, off:off + H])))
            B.stg_ev = stg_ev

        def issue_qkv_group(g, grp):
            if l == 0 and grp < 2:
                B.w(g, "gpsimd", B.scan_mm[6 + grp])
            if l == 1 and grp < 2:
                B.w(g, "gpsimd", B.w2mm_pred[7 - grp])
            c0 = grp * 2
            gi, par = B.ws.issue(g, [(
                B.wsb(B.ws.idx % 2)[:, 0:2048],
                bass.AP(B.d_wqkv.tensor, l * D * 3 * INNER + c0 * 128,
                        [[3 * INNER, 128], [128 * 3 * INNER, 8], [1, 256]]))])
            for c in range(2):
                grp_of[c0 + c] = (gi, par)
                ld_of[c0 + c] = B.ws.load_ev[gi]

        @blk.gpsimd
        def _(g):
            g = B.G
            issue_qkv_group(g, 0)
            issue_qkv_group(g, 1)

        def lhsT(c, kt):
            gi, par = grp_of[c]
            off = kt * 256 + (c % 2) * 128
            return B.wsb(par)[:, off:off + 128]

        def emit_chunk_mm(c):
            @blk.tensor
            def _(t, c=c):
                slot = c % 3
                B.w(t, "tensor", ld_of[c], *B.stg_ev)
                if c < 2 and l == 0:
                    B.w(t, "tensor", B.scan_evac[6 + c])
                if c >= 3:
                    B.wf(t, "tensor", cons_ev[c - 3])
                reg = reg_of(slot)
                for kt in range(NG):
                    for tc in range(2):
                        ins = t.matmul(reg[:, tc * 512:(tc + 1) * 512],
                                       lhsT(c, kt),
                                       B.stg[kt][:, tc * 512:(tc + 1) * 512],
                                       start=(kt == 0), stop=(kt == NG - 1))
                mm_ev[c] = B.ev(B.sT, ins)
                if c % 2 == 1:
                    B.ws.done(grp_of[c][0], mm_ev[c])

            if c % 2 == 1 and c + 3 < 12:
                @blk.gpsimd
                def _(g, c=c):
                    g = B.G
                    issue_qkv_group(g, (c + 3) // 2)

        for c in range(8):
            emit_chunk_mm(c)
            qk = 0 if c < 4 else 1
            g_ = c % 4

            @blk.vector
            def _(v, c=c):
                v = B.V
                B.wf(v, "vector", qevac_pred[c])
                if c >= 2:
                    B.w(v, "vector", cons_ev[c - 2])
                qraw_ev[c] = B.ev(B.sV, v.tensor_mul(
                    B.unb(SQB[c % 2], 1024), rawb(c), rawb(c)))

            @blk.tensor
            def _(t, c=c, qk=qk, g_=g_):
                B.w(t, "tensor", qraw_ev[c])
                p0 = 0 if qk == 0 else 32
                n2 = B.psB[p0:p0 + 8, 1024:2048]
                for tc in range(2):
                    ins = t.matmul(n2[:, tc * 512:(tc + 1) * 512],
                                   B.selsq[:, g_ * 8:(g_ + 1) * 8],
                                   B.unb(SQB[c % 2], 1024)[:, tc * 512:(tc + 1) * 512],
                                   start=(g_ == 0), stop=(g_ == 3),
                                   tile_position=(0, p0))
                cons_ev[c] = B.ev(B.sT, ins)
                if g_ == 3:
                    n2done[qk] = cons_ev[c]

        def vdm(g_):
            return B.probs[:, g_ * 1024:(g_ + 1) * 1024]

        for c in range(8, 12):
            cons_ev[c] = vevac_pred[c - 8]
            emit_chunk_mm(c)

        for qk in range(2):
            @blk.vector
            def _(v, qk=qk):
                v = B.V
                B.wf(v, "vector", sqrt_pred[qk])
                dst = B.rq[:] if qk == 0 else B.rk40[32:40, :]
                e = B.ev(B.sV, v.tensor_scalar_max(dst, dst, 1e-12))
                B.w(v, "vector", e)
                rows_ev[qk] = B.ev(B.sV, v.reciprocal(dst, dst))

        def emit_vtp(g_):
            @blk.tensor
            def _(t, g_=g_):
                B.wf(t, "tensor", cons_ev[8 + g_], cons_ev[11])
                if g_ >= 1:
                    B.wf(t, "tensor", B.vev_ev[g_ - 1])
                for tb in range(8):
                    ins = t.transpose(
                        B.psB[:, tb * 64:(tb + 1) * 64].bitcast(BF16),
                        vdm(g_)[:, tb * 128:(tb + 1) * 128],
                        B.identb[:])
                vtp_ev[g_] = B.ev(B.sT, ins)

        for qk in range(2):
            for g_ in range(4):
                c = qk * 4 + g_

                @blk.tensor
                def _(t, qk=qk, g_=g_, c=c):
                    B.w(t, "tensor", rows_ev[0], rows_ev[1],
                        n2done[0], n2done[1])
                    if c > 0 and apply_ev[c - 1] is not None:
                        B.w(t, "tensor", apply_ev[c - 1])
                    bc = B.psB[:, 1024:2048]
                    p0 = 0 if qk == 0 else 32
                    lhs_ = (B.selB[:, g_ * 128:(g_ + 1) * 128] if qk == 0
                            else B.selB2[32:40, g_ * 128:(g_ + 1) * 128])
                    srcr = (B.rq[:, :] if qk == 0 else B.rk40[32:40, :])
                    for tc in range(2):
                        ins = t.matmul(bc[:, tc * 512:(tc + 1) * 512],
                                       lhs_, srcr[:, tc * 512:(tc + 1) * 512],
                                       start=True, stop=True,
                                       tile_position=(p0, 0))
                    bc_ev[c] = B.ev(B.sT, ins)

                @blk.vector
                def _(v, qk=qk, g_=g_, c=c):
                    v = B.V
                    B.wf(v, "vector", B.qs8_ev)
                    B.w(v, "vector", bc_ev[c], qraw_ev[c])
                    bc = B.psB[:, 1024:2048]
                    scal = B.qs8 if qk == 0 else B.ks8
                    apply_ev[c] = B.ev(B.sV, v.scalar_tensor_tensor(
                        B.unb(QHB, 1024), rawb(c), scal[:, 0:1], bc,
                        ALU.mult, ALU.mult))

                @blk.tensor
                def _(t, c=c):
                    B.w(t, "tensor", apply_ev[c])
                    B.wf(t, "tensor", cons_ev[9] if c % 2 == 0 else cons_ev[10])
                    if c >= 2:
                        B.wf(t, "tensor", rotev_ev[c - 2])
                    pb = (c % 2) * 1024
                    for tc in range(2):
                        ins = t.matmul(
                            B.psA[:, pb + tc * 512:pb + (tc + 1) * 512],
                            B.jmatb[:], B.unb(QHB, 1024)[:, tc * 512:(tc + 1) * 512],
                            start=True, stop=True)
                    rotmm_ev[c] = B.ev(B.sT, ins)

                rotev_ev[c] = rot_pred[c]

                @blk.vector
                def _(v, qk=qk, g_=g_, c=c):
                    v = B.V
                    B.w(v, "vector", apply_ev[c], *B.tabadd_ev)
                    B.w(v, "vector", rotmm_ev[c])
                    tcb = B.tabsb[:, (0 if qk == 0 else 2 * H):]
                    tsb = B.tabsb[:, (H if qk == 0 else 3 * H):]
                    v.tensor_mul(B.unb(QHB, 1024), B.unb(QHB, 1024),
                                 tcb[:, 0:H])
                    B.wf(v, "vector", rotev_ev[c])
                    t2_ev[c] = B.ev(B.sV, v.tensor_mul(
                        B.unb(ROTB, 1024), B.unb(ROTB, 1024), tsb[:, 0:H]))
                    dstf = B.qp(g_) if qk == 0 else B.kp(g_)
                    e = B.ev(B.sV, v.tensor_add(dstf, B.unb(QHB, 1024),
                                                B.unb(ROTB, 1024)))
                    if qk == 0:
                        B.qprime_ev[g_] = e
                    else:
                        B.kprime_ev[g_] = e
                        if h == 0:
                            B.w(v, "vector", e)
                            B.ev(B.sV, v.tensor_copy(
                                B.halo_k[l][:, g_ * 512:(g_ + 1) * 512],
                                B.kp(g_)[:, 512:1024]))

                if qk == 0:
                    emit_vtp(g_)

        if h == 0:
            @blk.vector
            def _(v):
                v = B.V
                B.wf(v, "vector", B.vev_ev[3])
                for tb in range(4, 8):
                    ins = v.tensor_copy(
                        B.halo_v[l][:, (tb - 4) * 512:(tb - 3) * 512],
                        B.vtm(tb))
                B.ev(B.sV, ins)

        @blk.scalar
        def _(sc):
            sc = B.S
            B.w(sc, "scalar", B.lcols_ev)
            ev = B.ev(B.sS, sc.mul(B.qs8[:], B.qs8[:], 8.0))
            assert ev == B.qs8_ev, (ev, B.qs8_ev)
            for qk in range(2):
                for c in range(4 * qk, 4 * qk + 4):
                    B.w(sc, "scalar", mm_ev[c])
                    ev = B.ev(B.sS, B.Sn.activation(
                        rawb(c), reg_of(c % 3), AF.Identity))
                    assert ev == qevac_pred[c], (ev, qevac_pred[c])
                B.w(sc, "scalar", n2done[qk])
                dst = B.rq[:] if qk == 0 else B.rk40[32:40, :]
                p0 = 0 if qk == 0 else 32
                n2 = B.psB[p0:p0 + 8, 1024:2048]
                B.ev(B.sS, sc.activation(dst, n2, AF.Ln, bias=1e-30))
                ev = B.ev(B.sS, sc.activation(dst, dst, AF.Exp, scale=0.5))
                assert ev == sqrt_pred[qk], (ev, sqrt_pred[qk])
            for c in range(8, 12):
                B.w(sc, "scalar", mm_ev[c])
                ev = B.ev(B.sS, B.Sn.activation(vdm(c - 8), reg_of(c % 3),
                                                AF.Identity))
                assert ev == vevac_pred[c - 8], (ev, vevac_pred[c - 8])
            for c in range(8):
                B.w(sc, "scalar", rotmm_ev[c])
                pb = (c % 2) * 1024
                ev = B.ev(B.sS, B.Sn.activation(
                    B.unb(ROTB, 1024), B.psA[:, pb:pb + 1024], AF.Identity))
                assert ev == rot_pred[c], (ev, rot_pred[c])
                if c < 4:
                    B.w(sc, "scalar", vtp_ev[c])
                    for tb in range(8):
                        ss = B.Sn if tb < 7 else B.S
                        ins = ss.activation(
                            B.vtm(tb)[:, c * 128:(c + 1) * 128],
                            B.psB[:, tb * 64:(tb + 1) * 64].bitcast(BF16),
                            AF.Identity)
                    ev = B.ev(B.sS, ins)
                    assert ev == vev_pred[c], (ev, vev_pred[c])



def phase_attn(B, h, l, blk_=None):
    nc = B.nc

    with _maybe_block(nc, blk_) as blk:
        if h == 0 and B.sPC[l].n == 0:
            @blk.gpsimd
            def _(g):
                emit_precast(B, g, l, range(6))

        prev = {"exp": [None] * NH, "tev": [None] * NH, "oev": [None] * 4}
        for qc in range(8):
            gq = h * 8 + qc
            nkt = min(gq, 4) + 1
            w = nkt * 128
            kt_list = list(range(gq - nkt + 1, gq + 1))

            def simreg(hh):
                return (B.psA[:, 0:640] if hh % 2 == 0
                        else B.psA[:, 1024:1664])

            sb0 = B.sS.n
            exp_pred = [None] * NH
            tev_pred = [None] * NH
            for hh in range(NH):
                exp_pred[hh] = (B.sS, sb0 + hh + 1)
            nv = B.sV.n
            norm_pred = [None] * NH
            oev_pred = [None] * 4
            for pj in range(4):
                norm_pred[2 * pj] = (B.sV, nv + 3 * pj + 2)
                norm_pred[2 * pj + 1] = (B.sV, nv + 3 * pj + 3)
            nv += 12
            for pj in range(4):
                if pj >= 1:
                    oev_pred[pj - 1] = (B.sV, nv + 2)
                    nv += 2
                tev_pred[2 * pj] = (B.sV, nv + 1)
                tev_pred[2 * pj + 1] = (B.sV, nv + 2)
                nv += 2
            oev_pred[3] = (B.sV, nv + 2)
            sim_pred = [(B.sT, B.sT.n + hh + 1) for hh in range(NH)]
            ntt = B.sT.n + NH
            tp_pred = [None] * NH
            av_pred = [None] * 4
            for pj in range(4):
                tp_pred[2 * pj] = (B.sT, ntt + 1)
                tp_pred[2 * pj + 1] = (B.sT, ntt + 2)
                av_pred[pj] = (B.sT, ntt + 3)
                ntt += 3

            pv = prev

            @blk.tensor
            def _(t, qc=qc, gq=gq, nkt=nkt, w=w, kt_list=kt_list,
                  exp_pred=exp_pred, norm_pred=norm_pred, tev_pred=tev_pred,
                  oev_pred=oev_pred, sim_pred=sim_pred, tp_pred=tp_pred,
                  av_pred=av_pred, pv=pv, simreg=simreg):
                if qc == 0:
                    B.wf(t, "tensor", B.vevac_pred[1], B.vevac_pred[2],
                         B.vevac_pred[3], B.rot_pred[6], B.rot_pred[7])
                for hh in range(NH):
                    g_ = hh // 2
                    sub = hh % 2
                    B.w(t, "tensor", B.qprime_ev[g_], B.kprime_ev[g_])
                    if hh >= 2:
                        B.wf(t, "tensor", exp_pred[hh - 2])
                    elif pv["exp"][hh + 6] is not None:
                        B.w(t, "tensor", pv["exp"][hh + 6])
                    reg = simreg(hh)
                    lhs = B.qp(g_)[sub * 64:sub * 64 + 64,
                                   qc * 128:(qc + 1) * 128]
                    runs = []
                    for j in range(nkt):
                        tbk = kt_list[j]
                        if tbk >= h * 8:
                            srcid, s0 = 0, (tbk - h * 8) * 128
                        else:
                            srcid, s0 = 1, g_ * 512 + (tbk - 4) * 128
                        if runs and runs[-1][0] == srcid and \
                                runs[-1][2] + runs[-1][3] == s0:
                            runs[-1][3] += 128
                        else:
                            runs.append([srcid, j * 128, s0, 128])
                    bank_ops = {}
                    for srcid, col, s0, n in runs:
                        while n > 0:
                            seg = min(n, 512 - col % 512) if col % 512 else \
                                min(n, 512)
                            base = (B.kp(g_) if srcid == 0 else B.halo_k[l])
                            rhs = base[sub * 64:sub * 64 + 64, s0:s0 + seg]
                            bank_ops.setdefault(col // 512, []).append(
                                (col, lhs, rhs, (sub * 64, 0)))
                            col += seg
                            s0 += seg
                            n -= seg
                    if gq >= 4:
                        bank_ops[0].append((0, B.identb[:], B.stairb[:], None))
                    bank_ops[(w - 128) // 512].append(
                        (w - 128, B.identb[:], B.triagb[:], None))
                    for bk in sorted(bank_ops):
                        ops = bank_ops[bk]
                        for oi, (col, lh, rh, tpos) in enumerate(ops):
                            kw = {}
                            if tpos is not None:
                                kw["tile_position"] = tpos
                            ncol = rh.shape[-1] if tpos == (sub * 64, 0) \
                                else 128
                            ins = t.matmul(reg[:, col:col + ncol], lh, rh,
                                           start=(oi == 0),
                                           stop=(oi == len(ops) - 1), **kw)
                    ev = B.ev(B.sT, ins)
                    assert ev == sim_pred[hh], (ev, sim_pred[hh])
                for pj in range(4):
                    for hh in (2 * pj, 2 * pj + 1):
                        B.wf(t, "tensor", norm_pred[hh])
                        c0 = (hh % 2) * 512
                        if hh >= 2:
                            B.wf(t, "tensor", tev_pred[hh - 2])
                        elif pv["tev"][hh + 6] is not None:
                            B.w(t, "tensor", pv["tev"][hh + 6])
                        for j in range(nkt):
                            ins = t.transpose(
                                B.psB[:, c0 + j * 64:c0 + (j + 1) * 64].bitcast(BF16),
                                B.probs[:, hh * 640 + j * 128:
                                         hh * 640 + (j + 1) * 128],
                                B.identb[:])
                        ev = B.ev(B.sT, ins)
                        assert ev == tp_pred[hh], (ev, tp_pred[hh])
                    B.wf(t, "tensor", tev_pred[2 * pj], tev_pred[2 * pj + 1])
                    if qc == 0:
                        B.wf(t, "tensor", B.vev_ev[3])
                    if pj >= 1:
                        B.wf(t, "tensor", oev_pred[pj - 1])
                    elif pv["oev"][3] is not None:
                        B.w(t, "tensor", pv["oev"][3])
                    for j in range(nkt):
                        tbk = kt_list[j]
                        for sub in range(2):
                            hh = 2 * pj + sub
                            av = B.psB[sub * 64:sub * 64 + 64,
                                       1024 + sub * 512:1024 + sub * 512 + 128]
                            if tbk >= h * 8:
                                vs = B.vtm(tbk - h * 8)[:, hh * 64:(hh + 1) * 64]
                            else:
                                vs = B.halo_v[l][:, (tbk - 4) * 512 + hh * 64:
                                                 (tbk - 4) * 512 + (hh + 1) * 64]
                            ins = t.matmul(av, vs,
                                           B.attnT[hh % 2][:, j * 128:(j + 1) * 128],
                                           start=(j == 0), stop=(j == nkt - 1),
                                           tile_position=(0, sub * 64))
                    ev = B.ev(B.sT, ins)
                    assert ev == av_pred[pj], (ev, av_pred[pj])

            @blk.vector
            def _(v, qc=qc, gq=gq, nkt=nkt, w=w, exp_pred=exp_pred,
                  norm_pred=norm_pred, tev_pred=tev_pred,
                  oev_pred=oev_pred, sim_pred=sim_pred, tp_pred=tp_pred,
                  av_pred=av_pred, pv=pv, simreg=simreg):
                v = B.V
                for pj in range(4):
                    B.wf(v, "vector", exp_pred[2 * pj], exp_pred[2 * pj + 1])
                    rc = B.ev(B.sV, v.reciprocal(
                        B.rcp[:, 2 * pj:2 * pj + 2],
                        B.rsum[:, 2 * pj:2 * pj + 2]))
                    B.w(v, "vector", rc)
                    for hh in (2 * pj, 2 * pj + 1):
                        vv = B.Vn if hh % 2 == 0 else B.V
                        ev = B.ev(B.sV, vv.tensor_scalar_mul(
                            B.probs[:, hh * 640:hh * 640 + w],
                            B.probs[:, hh * 640:hh * 640 + w],
                            B.rcp[:, hh:hh + 1]))
                        assert ev == norm_pred[hh], (ev, norm_pred[hh])
                for pj in range(4):
                    if pj >= 1:
                        B.w(v, "vector", av_pred[pj - 1])
                        B.Vn.tensor_copy(
                            B.odm(pj - 1)[0:64, qc * 128:(qc + 1) * 128],
                            B.psB[0:64, 1024:1152])
                        ev = B.ev(B.sV, B.Vn.tensor_copy(
                            B.odm(pj - 1)[64:128, qc * 128:(qc + 1) * 128],
                            B.psB[64:128, 1536:1664]))
                        assert ev == oev_pred[pj - 1], (ev, oev_pred[pj - 1])
                    hh = 2 * pj
                    B.w(v, "vector", tp_pred[hh])
                    if hh == 0 and pv["oev"][3] is not None:
                        B.w(v, "vector", pv["oev"][3])
                    ev = B.ev(B.sV, B.Vn.tensor_copy(
                        B.attnT[0][:, 0:w],
                        B.psB[:, 0:w // 2].bitcast(BF16)))
                    assert ev == tev_pred[hh], (ev, tev_pred[hh])
                    B.w(v, "vector", tp_pred[hh + 1])
                    ev = B.ev(B.sV, v.tensor_copy(
                        B.attnT[1][:, 0:w],
                        B.psB[:, 512:512 + w // 2].bitcast(BF16)))
                    assert ev == tev_pred[hh + 1], (ev, tev_pred[hh + 1])
                B.w(v, "vector", av_pred[3])
                B.Vn.tensor_copy(B.odm(3)[0:64, qc * 128:(qc + 1) * 128],
                                 B.psB[0:64, 1024:1152])
                ev = B.ev(B.sV, v.tensor_copy(
                    B.odm(3)[64:128, qc * 128:(qc + 1) * 128],
                    B.psB[64:128, 1536:1664]))
                assert ev == oev_pred[3], (ev, oev_pred[3])

            @blk.scalar
            def _(sc, qc=qc, w=w, sim_pred=sim_pred, exp_pred=exp_pred,
                  tev_pred=tev_pred, tp_pred=tp_pred, oev_pred=oev_pred,
                  pv=pv, simreg=simreg):
                sc = B.S
                for hh in range(NH):
                    B.w(sc, "scalar", sim_pred[hh])
                    reg = simreg(hh)
                    ss = B.Sn if hh < NH - 1 else B.S
                    ev = B.ev(B.sS, ss.activation(
                        B.probs[:, hh * 640:hh * 640 + w], reg[:, 0:w],
                        AF.Exp, accum_out=B.rsum[:, hh:hh + 1]))
                    assert ev == exp_pred[hh], (ev, exp_pred[hh])

            prev = {"exp": exp_pred, "tev": tev_pred, "oev": oev_pred}
        B.oev_all = list(prev["oev"])





def phase_wo(B, h, l, blk_=None):
    nc = B.nc
    with _maybe_block(nc, blk_) as blk:
        grp_of = {}
        ld_of = {}
        mm_ev = [None] * 8
        res_ev = [None] * 8

        def issue_wo_group(g, grp):
            c0 = grp * 4
            gi, par = B.ws.issue(g, [(
                B.wsb(B.ws.idx % 2)[:, 0:2048],
                bass.AP(B.d_wo.tensor, l * INNER * D + c0 * 128,
                        [[D, 128], [128 * D, 4], [1, 512]]))])
            for c in range(4):
                grp_of[c0 + c] = (gi, par)
                ld_of[c0 + c] = B.ws.load_ev[gi]

        @blk.gpsimd
        def _(g):

            g = B.G
            issue_wo_group(g, 0)
            issue_wo_group(g, 1)

        B.res_ev = res_ev
        for m in range(8):
            @blk.tensor
            def _(t, m=m):
                B.w(t, "tensor", ld_of[m], *B.oev_all)
                if m >= 2:
                    B.w(t, "tensor", res_ev[m - 2])
                gi, par = grp_of[m]
                reg = B.psA[:, 0:1024] if m % 2 == 0 else B.psA[:, 1024:2048]
                for kt in range(4):
                    off = kt * 512 + (m % 4) * 128
                    for tc in range(2):
                        ins = t.matmul(reg[:, tc * 512:(tc + 1) * 512],
                                       B.wsb(par)[:, off:off + 128],
                                       B.odm(kt)[:, tc * 512:(tc + 1) * 512],
                                       start=(kt == 0), stop=(kt == 3))
                mm_ev[m] = B.ev(B.sT, ins)
                if m == 3:
                    assert grp_of[0][1] == 0
                    B.ws.done(grp_of[0][0], mm_ev[m])
                if m == 7:
                    B.ws.done(grp_of[4][0], mm_ev[m])
                    B.wo_done = mm_ev

            @blk.vector
            def _(v, m=m):

                v = B.V
                B.w(v, "vector", mm_ev[m])
                reg = B.psA[:, 0:1024] if m % 2 == 0 else B.psA[:, 1024:2048]
                res_ev[m] = B.ev(B.sV, v.tensor_add(
                    B.u[m][:, 1:H + 1], reg, B.u[m][:, 1:H + 1]))

        if h == 0:
            @blk.vector
            def _(v):

                v = B.V
                B.w(v, "vector", *res_ev)
                for g in range(NG):
                    ins = v.tensor_copy(B.hL[l][:, g:g + 1], B.u[g][:, H:H + 1])
                B.ev(B.sV, ins)


def phase_ln(B, h, l, blk_=None):
    nc = B.nc
    SQB = [4096, 4608]
    W1B = 0

    with _maybe_block(nc, blk_) as blk:
        @blk.sync
        def _(s):
            if h == 0:
                B.w(s, "sync", (B.sPC[l], B.sPC[l].n))
            pref = {}
            for j in range(4):
                pref[j] = ffn_w1_load(B, s, l, j,
                                      [B.wo_done[3 if j < 2 else 7]])
            B.ffn_pref = pref

        sq_ev = [None] * NG
        lnmm_ev = [None] * NG
        nrm_ev = [None] * NG
        B.stgx_ev = [None] * NG

        @blk.vector
        def _(v):

            v = B.V
            if h == 1:
                for g in range(NG):
                    ins = v.tensor_copy(B.u[g][:, 0:1], B.hL[l][:, g:g + 1])
                B.ev(B.sV, ins)
            else:
                for g in range(NG):
                    ins = v.memset(B.u[g][:, 0:1], 0.0)
                B.ev(B.sV, ins)
            stg_ev = []
            for g in range(NG):
                off = 1 if g < 4 else 0
                B.w(v, "vector", B.res_ev[g])
                stg_ev.append(B.ev(B.sV, v.tensor_copy(
                    B.stg[g][:], B.u[g][:, off:off + H])))
            B.stg_ev = stg_ev

        for g in range(NG):
            @blk.vector
            def _(v, g=g):

                v = B.V
                B.w(v, "vector", B.stg_ev[g])
                if g >= 2:
                    B.w(v, "vector", lnmm_ev[g - 2])
                sq_ev[g] = B.ev(B.sV, v.tensor_mul(
                    B.unb(SQB[g % 2], 1024), B.stg[g][:], B.stg[g][:]))

            @blk.tensor
            def _(t, g=g):
                B.w(t, "tensor", sq_ev[g])
                if g == 0:
                    B.w(t, "tensor", B.res_ev[6], B.res_ev[7])
                srow = B.psA[0:1, 0:1024]
                qrow = B.psA[0:1, 1024:2048]
                for tc in range(2):
                    t.matmul(srow[:, tc * 512:(tc + 1) * 512], B.onesb[:],
                             B.stg[g][:, tc * 512:(tc + 1) * 512],
                             start=(g == 0), stop=(g == NG - 1))
                    ins = t.matmul(qrow[:, tc * 512:(tc + 1) * 512], B.onesb[:],
                                   B.unb(SQB[g % 2], 1024)[:, tc * 512:(tc + 1) * 512],
                                   start=(g == 0), stop=(g == NG - 1))
                lnmm_ev[g] = B.ev(B.sT, ins)

        @blk.vector
        def _(v):

            v = B.V
            B.w(v, "vector", lnmm_ev[NG - 1])
            srow = B.psA[0:1, 0:1024]
            qrow = B.psA[0:1, 1024:2048]
            v.tensor_scalar_mul(srow, srow, 1.0 / D)
            v.tensor_scalar_mul(qrow, qrow, 1.0 / D)
            v.tensor_copy(B.rowsSB[:, 0:1024], srow)
            v.tensor_tensor(B.rowsSB[:, 1024:2048], B.rowsSB[:, 0:1024],
                            B.rowsSB[:, 0:1024], ALU.mult)
            B.var_ev = B.ev(B.sV, v.tensor_sub(qrow, qrow,
                                               B.rowsSB[:, 1024:2048]))

        @blk.scalar
        def _(sc):

            sc = B.S
            B.w(sc, "scalar", B.var_ev)
            qrow = B.psA[0:1, 1024:2048]
            e = B.ev(B.sS, sc.activation(qrow, qrow, AF.Ln, bias=1e-5))
            B.w(sc, "scalar", e)
            B.rstd_ev = B.ev(B.sS, sc.activation(
                B.rowsSB[:, 0:1024], qrow, AF.Exp, scale=-0.5))

        @blk.vector
        def _(v):

            v = B.V
            B.w(v, "vector", B.rstd_ev)
            srow = B.psA[0:1, 0:1024]
            B.brow_ev = B.ev(B.sV, v.scalar_tensor_tensor(
                B.rowsSB[:, 1024:2048], srow, -1.0, B.rowsSB[:, 0:1024],
                ALU.mult, ALU.mult))

        @blk.tensor
        def _(t):
            B.w(t, "tensor", B.brow_ev)
            for tc in range(2):
                t.matmul(B.psA[:, tc * 512:(tc + 1) * 512], B.onesr[:],
                         B.rowsSB[:, tc * 512:(tc + 1) * 512],
                         start=True, stop=True)
                ins = t.matmul(B.psA[:, 1024 + tc * 512:1024 + (tc + 1) * 512],
                               B.onesr[:],
                               B.rowsSB[:, 1024 + tc * 512:1024 + (tc + 1) * 512],
                               start=True, stop=True)
            B.lnbc_ev = B.ev(B.sT, ins)

        stgx_pred = [(B.sS, B.sS.n + g + 1) for g in range(NG)]

        @blk.vector
        def _(v):
            v = B.V
            B.w(v, "vector", B.lnbc_ev)
            for g in range(NG):
                off = 1 if g < 4 else 0
                hsl = B.u[g][:, off:off + H]
                w1 = B.un(W1B + (g % 2) * 1024, 1024)
                if g >= 2:
                    B.wf(v, "vector", stgx_pred[g - 2])
                v.tensor_mul(w1, hsl, B.psA[:, 0:1024])
                nrm_ev[g] = B.ev(B.sV, v.tensor_add(
                    w1, w1, B.psA[:, 1024:2048]))

        @blk.scalar
        def _(sc):
            sc = B.S
            for g in range(NG):
                B.w(sc, "scalar", nrm_ev[g])
                ev = B.ev(B.sS, sc.activation(
                    B.stg[g][:], B.un(W1B + (g % 2) * 1024, 1024), AF.Identity,
                    bias=B.lbc[:, g:g + 1], scale=B.lgc[:, g:g + 1]))
                assert ev == stgx_pred[g], (ev, stgx_pred[g])
                B.stgx_ev[g] = ev


def h2view(B, j):
    if j < 16:
        return B.bview(j // 2)[:, (j % 2) * H:(j % 2 + 1) * H]
    return B.unb((j - 16) * 512, 1024)


def ffn_w1_load(B, s, l, j, avail):
    o, ln = HCH[j]
    u = j % 4
    sem = B.sFu[u]
    B.wf(s, "sync", *avail)
    woff = B.pre_off[(l, "w1", j)]
    B.ev(sem, s.dma_start(
        out=B.wsu(u)[:, 0:8 * ln],
        in_=bass.AP(B.d_wpre.tensor, woff, [[8 * ln, 128], [1, 8 * ln]])), 16)
    B.ev(sem, s.dma_start(
        out=B.wsu(u)[:, 1024:1024 + 8 * ln],
        in_=bass.AP(B.d_wpre.tensor, woff + 1024 * ln,
                    [[8 * ln, 128], [1, 8 * ln]])), 16)
    return B.mark(sem)


def ffn_w2_load(B, s, l, m, avail):
    u0 = 2 if m % 2 == 0 else 0
    sem = B.sFu[u0]
    B.wf(s, "sync", *avail)
    base = B.pre_off[(l, "w2", m)]
    for c0, c1 in ((0, 896), (896, 1792), (1792, 2688)):
        B.ev(sem, s.dma_start(
            out=B.wsu2(u0)[:, c0:c1],
            in_=bass.AP(B.d_wpre.tensor, base + c0,
                        [[2688, 128], [1, c1 - c0]])), 16)
    B.ev(sem, s.dma_start(
        out=B.wsu2(u0)[0:42, 2688:2816],
        in_=bass.AP(B.d_wpre.tensor, B.pre_off[(l, "w2t", m)],
                    [[128, 42], [1, 128]])), 16)
    return B.mark(sem)


def phase_ffn(B, h, l, blk_=None):
    nc = B.nc
    GEL = [3072, 3584]

    with _maybe_block(nc, blk_) as blk:
        ld_of = {}
        w2ld = {}

        tbase, sbase, vbase = B.sT.n, B.sS.n, B.sV.n
        mm_pred = [(B.sT, tbase + j + 1) for j in range(NHCH)]
        w2mm_pred = [(B.sT, tbase + NHCH + m + 1) for m in range(8)]
        B.w2mm_pred = w2mm_pred
        B.ffn_done_evs = None
        gel_pred = [(B.sS, sbase + j + 1) for j in range(NHCH)]
        gate_pred = [(B.sV, vbase + j + 1) for j in range(NHCH)]
        w2res_pred = [(B.sV, vbase + NHCH + m + 1) for m in range(8)]
        B.ffn_done_evs = (w2res_pred[7], (B.sS, sbase + NHCH))

        @blk.sync
        def _(s):
            for j in range(4):
                ld_of[j] = B.ffn_pref[j]
            for j in range(4, NHCH):
                ld_of[j] = ffn_w1_load(B, s, l, j, [mm_pred[j - 4]])
            for m in range(8):
                if m == 0:
                    avail = [mm_pred[18], mm_pred[19]]
                elif m == 1:
                    avail = [mm_pred[20], mm_pred[21]]
                else:
                    avail = [w2mm_pred[m - 2]]
                w2ld[m] = ffn_w2_load(B, s, l, m, avail)

        @blk.tensor
        def _(t):
            for j in range(NHCH):
                o, ln = HCH[j]
                B.w(t, "tensor", ld_of[j])
                if j == 0:
                    pass
                elif j == 1:
                    B.w(t, "tensor", B.stgx_ev[7])
                else:
                    B.w(t, "tensor", *B.stgx_ev)
                if j >= 2:
                    B.wf(t, "tensor", gate_pred[j - 2])
                wu = B.wsu(j % 4)
                rA = B.psB[:, 0:1024] if j % 2 == 0 else B.psA[:, 0:1024]
                rG = B.psB[:, 1024:2048] if j % 2 == 0 else B.psA[:, 1024:2048]
                for kt in range(NG):
                    if j == 0:
                        B.w(t, "tensor", B.stgx_ev[kt])
                    for tc in range(2):
                        t.matmul(rA[0:ln, tc * 512:(tc + 1) * 512],
                                 wu[:, kt * ln:(kt + 1) * ln],
                                 B.stg[kt][:, tc * 512:(tc + 1) * 512],
                                 start=(kt == 0), stop=(kt == NG - 1))
                        ins = t.matmul(
                            rG[0:ln, tc * 512:(tc + 1) * 512],
                            wu[:, 1024 + kt * ln:1024 + (kt + 1) * ln],
                            B.stg[kt][:, tc * 512:(tc + 1) * 512],
                            start=(kt == 0), stop=(kt == NG - 1))
                ev = B.ev(B.sT, ins)
                assert ev == mm_pred[j], (ev, mm_pred[j])
            for m in range(8):
                B.w(t, "tensor", w2ld[m])
                B.wf(t, "tensor", gate_pred[NHCH - 1])
                if m >= 2:
                    B.wf(t, "tensor", w2res_pred[m - 2])
                wu2 = B.wsu2(2 if m % 2 == 0 else 0)
                reg = B.psA[:, 0:1024] if m % 2 == 0 else B.psA[:, 1024:2048]
                for j in range(NHCH):
                    o, ln = HCH[j]
                    for tc in range(2):
                        ins = t.matmul(reg[:, tc * 512:(tc + 1) * 512],
                                       wu2[0:ln, j * 128:j * 128 + 128],
                                       h2view(B, j)[0:ln, tc * 512:(tc + 1) * 512],
                                       start=(j == 0), stop=(j == NHCH - 1))
                ev = B.ev(B.sT, ins)
                assert ev == w2mm_pred[m], (ev, w2mm_pred[m])

        @blk.scalar
        def _(sc):
            sc = B.S
            for j in range(NHCH):
                o, ln = HCH[j]
                B.w(sc, "scalar", mm_pred[j])
                if j >= 2:
                    B.wf(sc, "scalar", gate_pred[j - 2])
                rG = B.psB[:, 1024:2048] if j % 2 == 0 else B.psA[:, 1024:2048]
                ev = B.ev(B.sS, sc.activation(
                    B.unb(GEL[j % 2], 1024)[0:ln, :], rG[0:ln, :], AF.Gelu,
                    bias=B.b1g[0:ln, j:j + 1]))
                assert ev == gel_pred[j], (ev, gel_pred[j])

        @blk.vector
        def _(v):
            v = B.V
            for j in range(NHCH):
                o, ln = HCH[j]
                B.w(v, "vector", gel_pred[j])
                rA = B.psB[:, 0:1024] if j % 2 == 0 else B.psA[:, 0:1024]
                ev = B.ev(B.sV, v.scalar_tensor_tensor(
                    h2view(B, j)[0:ln, :], rA[0:ln, :], B.b1a[0:ln, j:j + 1],
                    B.unb(GEL[j % 2], 1024)[0:ln, :], ALU.add, ALU.mult))
                assert ev == gate_pred[j], (ev, gate_pred[j])
            for m in range(8):
                B.w(v, "vector", w2mm_pred[m])
                reg = B.psA[:, 0:1024] if m % 2 == 0 else B.psA[:, 1024:2048]
                ev = B.ev(B.sV, v.scalar_tensor_tensor(
                    B.u[m][:, 1:H + 1], reg, B.b2c[:, m:m + 1],
                    B.u[m][:, 1:H + 1], ALU.add, ALU.add))
                assert ev == w2res_pred[m], (ev, w2res_pred[m])



def phase_final(B, h, blk_=None):
    nc = B.nc
    t0 = h * H
    XTI = [4096, 5120]

    with _maybe_block(nc, blk_) as blk:
        sum_ev = [None] * NG
        tp_ev = [None] * 8
        add_ev = [None] * 8
        x_ev = [None] * 8
        st_ev = [None] * 8

        for g in range(NG):
            @blk.vector
            def _(v, g=g):

                v = B.V
                B.wf(v, "vector", B.w2mm_pred[7])
                sum_ev[g] = B.ev(B.sV, v.tensor_add(
                    B.Bb[g][:, 0:H], B.parkSB[g][:], B.u[g][:, 1:H + 1]))

        def issue_xf(s, tb):
            if tb >= 2:
                B.w(s, "sync", st_ev[tb - 2])
            sem = B.sXp[tb % 2]
            B.ev(sem, s.dma_start(
                out=B.un(XTI[tb % 2], 1024),
                in_=B.d_x[t0 + tb * 128:t0 + (tb + 1) * 128, :]), 16)
            x_ev[tb] = B.mark(sem)

        @blk.sync
        def _(s):
            issue_xf(s, 0)
            issue_xf(s, 1)

        for tb in range(8):
            @blk.tensor
            def _(t, tb=tb):
                B.w(t, "tensor", *sum_ev)
                if tb >= 2:
                    B.w(t, "tensor", add_ev[tb - 2])
                reg = B.psb(0 if tb % 2 == 0 else 4, 1024)
                for g in range(NG):
                    ins = t.transpose(reg[:, g * 128:(g + 1) * 128],
                                      B.Bb[g][:, tb * 128:(tb + 1) * 128],
                                      B.ident32[:])
                tp_ev[tb] = B.ev(B.sT, ins)

            @blk.vector
            def _(v, tb=tb):

                v = B.V
                B.w(v, "vector", tp_ev[tb], x_ev[tb])
                if tb >= 2:
                    B.w(v, "vector", st_ev[tb - 2])
                reg = B.psb(0 if tb % 2 == 0 else 4, 1024)
                add_ev[tb] = B.ev(B.sV, v.tensor_add(
                    B.un(XTI[tb % 2], 1024), B.un(XTI[tb % 2], 1024), reg))

            @blk.sync
            def _(s, tb=tb):
                B.w(s, "sync", add_ev[tb])
                sem = B.sPp[tb % 2]
                B.ev(sem, s.dma_start(
                    out=B.d_out[t0 + tb * 128:t0 + (tb + 1) * 128, :],
                    in_=B.un(XTI[tb % 2], 1024)), 16)
                st_ev[tb] = B.mark(sem)
                if tb + 2 < 8:
                    issue_xf(s, tb + 2)
                if tb == 7:
                    B.final_done = (B.mark(B.sPp[0]), B.mark(B.sPp[1]))


def phase_dbg_qkv(B, h):
    t0 = h * H
    with B.nc.Block() as blk:
        @blk.gpsimd
        def _(g):
            for gg in range(4):
                B.ev(B.sXG, g.dma_start(
                    out=B.d_dbg["dbg_q"][gg * 128:(gg + 1) * 128, t0:t0 + H],
                    in_=B.qp(gg)), 16)
                B.ev(B.sXG, g.dma_start(
                    out=B.d_dbg["dbg_k"][gg * 128:(gg + 1) * 128, t0:t0 + H],
                    in_=B.kp(gg)), 16)
            for tb in range(8):
                B.ev(B.sXG, g.dma_start(
                    out=B.d_dbg["dbg_v"][(h * 8 + tb) * 128:(h * 8 + tb + 1) * 128, :],
                    in_=B.vtm(tb)), 16)
            g.wait_ge(B.sXG.h, B.sXG.n)


def phase_dbg_o(B, h):
    t0 = h * H
    with B.nc.Block() as blk:
        @blk.gpsimd
        def _(g):
            for gg in range(4):
                B.ev(B.sXG, g.dma_start(
                    out=B.d_dbg["dbg_o"][gg * 128:(gg + 1) * 128, t0:t0 + H],
                    in_=B.odm(gg)), 16)
            g.wait_ge(B.sXG.h, B.sXG.n)


def phase_dbg_u(B, h, name):
    nc = B.nc
    t0 = h * H
    with nc.Block() as blk:
        @blk.sync
        def _(s):
            for g in range(NG):
                B.ev(B.sP, s.dma_start(
                    out=B.d_dbg[name][g * 128:(g + 1) * 128, t0:t0 + H],
                    in_=B.u[g][:, 1:H + 1]), 16)
            B.w(s, "sync", B.mark(B.sP))


_BUILT = {}


def get_built(debug=False):
    if debug not in _BUILT:
        _BUILT[debug] = build(debug=debug)
    return _BUILT[debug]


def make_in_maps(B, inputs):
    x = np.ascontiguousarray(np.asarray(inputs["L0_attn_x"], dtype=np.float32))
    shared = {
        "ssm_a": inputs["L0_attn_a"], "ssm_b": inputs["L0_attn_b"],
        "b_out": inputs["L0_attn_b_out"],
        "wqkv": inputs["L0_ffn_wqkv"], "q_scale": inputs["L0_ffn_q_scale"],
        "k_scale": inputs["L0_ffn_k_scale"], "wo": inputs["L0_ffn_wo"],
        "ln_g": inputs["L0_ffn_ln_g"], "ln_b": inputs["L0_ffn_ln_b"],
        "b1": inputs["L0_ffn_b1"], "b2": inputs["L0_ffn_b2"],
    }
    shared = {k: np.ascontiguousarray(np.asarray(v, np.float32))
              for k, v in shared.items()}
    wout = np.asarray(inputs["L0_attn_w_out"], np.float32)
    shared["woutp"] = np.ascontiguousarray(np.concatenate(
        [wout[ec * 128:(ec + 1) * 128, :].reshape(128, 8, 128)
         .transpose(2, 1, 0).ravel() for ec in range(8)]))
    w1 = np.asarray(inputs["L0_ffn_w1"], np.float32)
    w2 = np.asarray(inputs["L0_ffn_w2"], np.float32)
    blocks = []
    for l in range(NL):
        for j in range(NHCH):
            o, ln = HCH[j]
            for b in range(2):
                blocks.append(w1[l][:, b * HID + o:b * HID + o + ln]
                              .reshape(8, 128, ln).transpose(1, 0, 2).ravel())
    shared["w1p"] = np.ascontiguousarray(np.concatenate(blocks))
    blocks = []
    for l in range(NL):
        for m in range(8):
            blocks.append(w2[l][:21 * 128, m * 128:(m + 1) * 128]
                          .reshape(21, 128, 128).transpose(1, 0, 2).ravel())
            blocks.append(w2[l][21 * 128:, m * 128:(m + 1) * 128].ravel())
    shared["w2p"] = np.ascontiguousarray(np.concatenate(blocks))
    shared.update({k: np.ascontiguousarray(v) for k, v in B.consts.items()})
    in_maps = []
    for c in range(8):
        m = dict(shared)
        m["x"] = np.ascontiguousarray(x[c])
        in_maps.append(m)
    return in_maps


def kernel(**inputs):
    B = get_built(debug=False)
    in_maps = make_in_maps(B, inputs)
    res = run_bass_kernel_spmd(B.nc, in_maps, core_ids=list(range(8)))
    return np.stack([r["out"] for r in res.results], axis=0)


# version 42 (final)
# speedup vs baseline: 1.0081x; 1.0081x over previous
import sys
if '/opt/trn_rl_repo' not in sys.path:
    sys.path.insert(0, '/opt/trn_rl_repo')

import numpy as np
import ml_dtypes
import concourse.bass as bass
import concourse.mybir as mybir
from concourse.bass_utils import run_bass_kernel_spmd
from contextlib import ExitStack

F32 = mybir.dt.float32
BF16 = mybir.dt.bfloat16
AF = mybir.ActivationFunctionType
ALU = mybir.AluOpType
AXX = mybir.AxisListType.X
BF = ml_dtypes.bfloat16

T = 2048
D = 1024
H = 1024
NG = 8
NH = 8
DH = 64
INNER = 512
HID = 2730
W1N = 2 * HID
WIN = 512
NL = 2
MASKV = -float(2.0 ** 100)
NHCH = 22
HCH = [(j * 128, min(128, HID - j * 128)) for j in range(NHCH)]
W1PN = 2 * 1024 * HID
W2PN = 8 * (128 * 21 * 128 + 42 * 128)
LAYER_PN = W1PN + W2PN


def _host_consts():
    c = {}
    c["c_ident32"] = np.eye(128, dtype=np.float32)
    c["c_identb"] = np.eye(128).astype(BF)
    st = np.zeros((128, 128), np.float32)
    tr = np.zeros((128, 128), np.float32)
    for i in range(128):
        st[i, :i] = MASKV
        tr[i, i + 1:] = MASKV
    c["c_stair"] = st
    c["c_triag"] = tr
    c["c_stairb"] = st.astype(BF)
    c["c_triagb"] = tr.astype(BF)
    sq = np.zeros((128, 32), np.float64)
    for g in range(4):
        for p in range(128):
            sq[p, g * 8 + 2 * g + p // 64] = 1.0
    c["c_selsq"] = sq.astype(BF)
    sb_ = np.zeros((8, 4 * 128), np.float32)
    for g in range(4):
        for p in range(128):
            sb_[2 * g + p // 64, g * 128 + p] = 1.0
    c["c_selB"] = sb_
    sb2 = np.zeros((40, 4 * 128), np.float32)
    sb2[32:40] = sb_
    c["c_selB2"] = sb2
    c["c_onesb"] = np.ones((128, 1)).astype(BF)
    jm = np.zeros((128, 128), np.float32)
    for k_ in range(128):
        m_ = (k_ // 64) * 64 + ((k_ % 64) + 32) % 64
        jm[k_, m_] = 1.0
    c["c_jmat"] = jm
    c["c_jmatb"] = jm.astype(BF)
    sc_vals = [1e-5, -0.5, 1.0 / 1024.0, -1.0 / 1024.0, -1.0, 8.0, 0.5, 1e-30]
    c["c_scal"] = np.tile(np.array(sc_vals, np.float32)[None, :], (128, 1))
    c["c_onesr"] = np.ones((1, 128), np.float32)
    j = np.arange(0, DH, 2, dtype=np.float64)
    inv = 1.0 / (10000.0 ** (j / DH))
    t = np.arange(T, dtype=np.float64)
    fr = t[:, None] * inv[None, :]
    base = (j + 0.4 * DH) / (1.4 * DH)
    power = (t - T // 2) / (WIN // 2)
    scale = base[None, :] ** power[:, None]
    cos, sin = np.cos(fr), np.sin(fr)

    def expand(m):
        return np.concatenate([m, m], axis=1).T
    sgn = np.where(np.arange(64) < 32, -1.0, 1.0)[:, None]
    tabs = np.stack([expand(cos * scale), expand(sin * scale) * sgn,
                     expand(cos / scale), expand(sin / scale) * sgn])
    tabs = np.concatenate([tabs, tabs], axis=1)
    hi = tabs.astype(np.float32)
    lo = (tabs - hi.astype(np.float64)).astype(np.float32)
    c["c_tabhi"] = hi
    c["c_tablo"] = lo
    return c


class Sem:
    __slots__ = ("h", "n")

    def __init__(self, stack, nc, name):
        self.h = stack.enter_context(nc.semaphore(name))
        self.n = 0


class KB:
    def __init__(self, debug=False):
        self.debug = debug
        nc = bass.Bass("TRN2", target_bir_lowering=False)
        self.nc = nc
        st = ExitStack()
        self.stack = st

        dt = nc.dram_tensor
        self.d_x = dt("x", [T, D], F32, kind="ExternalInput").ap()
        self.d_a = dt("ssm_a", [D], F32, kind="ExternalInput").ap()
        self.d_b = dt("ssm_b", [D], F32, kind="ExternalInput").ap()
        self.d_woutp = dt("woutp", [D * D], F32, kind="ExternalInput").ap()
        self.d_bout = dt("b_out", [D], F32, kind="ExternalInput").ap()
        self.d_wqkv = dt("wqkv", [NL, D, 3 * INNER], F32, kind="ExternalInput").ap()
        self.d_qs = dt("q_scale", [NL, DH], F32, kind="ExternalInput").ap()
        self.d_ks = dt("k_scale", [NL, DH], F32, kind="ExternalInput").ap()
        self.d_wo = dt("wo", [NL, INNER, D], F32, kind="ExternalInput").ap()
        self.d_lng = dt("ln_g", [NL, D], F32, kind="ExternalInput").ap()
        self.d_lnb = dt("ln_b", [NL, D], F32, kind="ExternalInput").ap()
        self.d_w1p = dt("w1p", [NL * W1PN], F32, kind="ExternalInput").ap()
        self.d_b1 = dt("b1", [NL, W1N], F32, kind="ExternalInput").ap()
        self.d_w2p = dt("w2p", [NL * W2PN], F32, kind="ExternalInput").ap()
        self.d_b2 = dt("b2", [NL, D], F32, kind="ExternalInput").ap()
        self.consts = _host_consts()
        for k, v in self.consts.items():
            setattr(self, "d" + k, dt(k, list(v.shape),
                                      BF16 if v.dtype == BF else F32,
                                      kind="ExternalInput").ap())
        self.d_out = dt("out", [T, D], F32, kind="ExternalOutput").ap()
        self.d_park = dt("park", [NG * 128, T], F32, kind="Internal").ap()
        self.pre_off = {}
        off = 0
        for l in range(NL):
            for j in range(NHCH):
                _, ln = HCH[j]
                self.pre_off[(l, "w1", j)] = off
                off += 2 * 1024 * ln
            for m in range(8):
                self.pre_off[(l, "w2", m)] = off
                off += 128 * 21 * 128
                self.pre_off[(l, "w2t", m)] = off
                off += 42 * 128
        self.pre_off["wout"] = off
        off += D * D
        self.d_wpre = dt("wpre", [off], BF16, kind="Internal").ap()
        if debug:
            self.d_dbg = {}
            for nm, shape in [("dbg_xn", [NG * 128, T]),
                              ("dbg_q", [4 * 128, T]), ("dbg_k", [4 * 128, T]),
                              ("dbg_v", [16 * 128, INNER]),
                              ("dbg_o", [4 * 128, T]),
                              ("dbg_u1", [NG * 128, T]),
                              ("dbg_qh", [4 * 128, H]),
                              ("dbg_rot", [4 * 128, H]),
                              ("dbg_rq", [8, H])]:
                self.d_dbg[nm] = dt(nm, shape, F32, kind="ExternalOutput").ap()

        def sb(name, shape, dty):
            return st.enter_context(nc.sbuf_tensor(name, shape, dty))
        self.u = [sb(f"u{g}", [128, H + 1], F32) for g in range(NG)]
        self.parkSB = [sb(f"parkSB{g}", [128, H], BF16) for g in range(NG)]
        self.stg = [sb(f"stg{g}", [128, H], BF16) for g in range(NG)]
        self.Bb = [sb(f"Bb{i}", [128, H], F32) for i in range(NG)]
        self.tabsb = sb("tabsb", [128, 4 * H], BF16)
        self.UN = sb("UN", [128, 6400], F32)
        self.WS = sb("WS", [128, 4096], F32)
        self.probs = sb("probs", [128, NH * 640], BF16)
        self.attnT = [sb(f"attnT{i}", [128, 640], BF16) for i in range(2)]
        self.halo_k = [sb(f"halok{l}", [128, 4 * 512], BF16) for l in range(NL)]
        self.halo_v = [sb(f"halov{l}", [128, 4 * 512], BF16) for l in range(NL)]
        self.rowsSB = sb("rowsSB", [1, 2048], F32)
        self.rq = sb("rq", [8, H], F32)
        self.rk40 = sb("rk40", [40, H], F32)
        self.acol = sb("acol", [128, 8], F32)
        self.ah8 = sb("ah8", [128, 8], F32)
        self.bcol = sb("bcol", [128, 8], F32)
        self.boutc = sb("boutc", [128, 8], F32)
        self.qs8 = sb("qs8", [128, 1], F32)
        self.ks8 = sb("ks8", [128, 1], F32)
        self.lgc = sb("lgc", [128, 8], F32)
        self.lbc = sb("lbc", [128, 8], F32)
        self.b1a = sb("b1a", [128, 22], F32)
        self.b1g = sb("b1g", [128, 22], F32)
        self.b2c = sb("b2c", [128, 8], F32)
        self.nmax = sb("nmax", [128, 8], F32)
        self.rsum = sb("rsum", [128, 8], F32)
        self.rcp = sb("rcp", [128, 8], F32)
        self.carry = sb("carry", [128, 8], F32)
        self.hQ = [sb(f"hQ{l}", [128, 8], F32) for l in range(NL)]
        self.hL = [sb(f"hL{l}", [128, 8], F32) for l in range(NL)]
        self.ident32 = sb("ident32", [128, 128], F32)
        self.identb = sb("identb", [128, 128], BF16)
        self.stair = sb("stair", [128, 128], F32)
        self.triag = sb("triag", [128, 128], F32)
        self.stairb = sb("stairb", [128, 128], BF16)
        self.triagb = sb("triagb", [128, 128], BF16)
        self.selsq = sb("selsq", [128, 32], BF16)
        self.selB = sb("selB", [8, 4 * 128], F32)
        self.selB2 = sb("selB2", [40, 4 * 128], F32)
        self.onesb = sb("onesb", [128, 1], BF16)
        self.onesr = sb("onesr", [1, 128], F32)
        self.jmat = sb("jmat", [128, 128], F32)
        self.jmatb = sb("jmatb", [128, 128], BF16)
        self.scal = sb("scal", [128, 8], F32)
        for i, v in enumerate([1e-5, -0.5, 1.0 / 1024.0, -1.0 / 1024.0,
                               -1.0, 8.0, 0.5, 1e-30]):
            nc.const_aps.register(F32, v, self.scal[:, i:i + 1])

        self.psA = st.enter_context(nc.psum_tensor("psA", [128, 2048], F32))
        self.psB = st.enter_context(nc.psum_tensor("psB", [128, 2048], F32))

        self.sW = Sem(st, nc, "sW")
        self.sX = Sem(st, nc, "sX")
        self.sXG = Sem(st, nc, "sXG")
        self.sXp = [Sem(st, nc, "sXp0"), Sem(st, nc, "sXp1")]
        self.sWp = [Sem(st, nc, "sWp0"), Sem(st, nc, "sWp1")]
        self.sDp = [Sem(st, nc, "sDp0"), Sem(st, nc, "sDp1")]
        self.sPC = [Sem(st, nc, "sPC0"), Sem(st, nc, "sPC1")]
        self.sFu = [Sem(st, nc, f"sFu{u}") for u in range(4)]
        self.sPW = Sem(st, nc, "sPW")
        self.sPp = [Sem(st, nc, "sPp0"), Sem(st, nc, "sPp1")]
        self.sT = Sem(st, nc, "sT")
        self.sV = Sem(st, nc, "sV")
        self.sS = Sem(st, nc, "sS")
        self.sG = Sem(st, nc, "sG")
        self.sP = Sem(st, nc, "sP")
        self._waited = {}
        self._auto_evs = {}
        self._auto_refs = []
        self.V = EngProxy(self, nc.vector, "vector", self.sV)
        self.S = EngProxy(self, nc.scalar, "scalar", self.sS)
        self.G = EngProxy(self, nc.gpsimd, "gpsimd", self.sG)
        self.Vn = EngProxy(self, nc.vector, "vector", self.sV, nosync=True)
        self.Sn = EngProxy(self, nc.scalar, "scalar", self.sS, nosync=True)

    def ev(self, sem, instr, k=1):
        ae = self._auto_evs.get(id(instr))
        if ae is not None:
            assert ae[0] is sem, "event sem mismatch with auto-sync sem"
            return ae
        instr.then_inc(sem.h, k)
        sem.n += k
        return (sem, sem.n)

    def mark(self, sem):
        return (sem, sem.n)

    def w(self, eng, engname, *events, future=False):
        d = self._waited.setdefault(engname, {})
        need = {}
        for e in events:
            if e is None:
                continue
            sem, n = e
            assert future or n <= sem.n, \
                f"{engname} waits future event ({n} > {sem.n})"
            if n > 0:
                need[sem] = max(need.get(sem, 0), n)
        for sem, n in need.items():
            if d.get(sem, -1) >= n:
                continue
            eng.wait_ge(sem.h, n)
            d[sem] = n

    def wf(self, eng, engname, *events):
        self.w(eng, engname, *events, future=True)

    def bview(self, i):
        return self.Bb[i][:, 0:H].bitcast(BF16)

    def qp(self, g):
        return self.bview(g)[:, 0:H]

    def kp(self, g):
        return self.bview(g)[:, H:2 * H]

    def vtm(self, tb):
        return self.bview(4 + tb // 4)[:, (tb % 4) * 512:(tb % 4 + 1) * 512]

    def odm(self, g):
        return self.bview(6 + g // 2)[:, (g % 2) * H:(g % 2 + 1) * H]

    def un(self, off, n):
        return self.UN[:, off:off + n]

    def unb(self, off_f32, n_bf16):
        return self.UN[:, off_f32:off_f32 + (n_bf16 + 1) // 2].bitcast(BF16)[:, 0:n_bf16]

    def wsb(self, par):
        return self.WS[:, par * 2048:(par + 1) * 2048].bitcast(BF16)

    def wsu(self, u):
        return self.WS[:, u * 1024:(u + 1) * 1024].bitcast(BF16)

    def wsu2(self, u0):
        return self.WS[:, u0 * 1024:(u0 + 2) * 1024].bitcast(BF16)

    def psb(self, bank, cols=512):
        ps = self.psA if bank < 4 else self.psB
        c0 = (bank % 4) * 512
        return ps[:, c0:c0 + cols]



_DATA_OPS = {
    "tensor_copy", "tensor_tensor", "tensor_add", "tensor_sub", "tensor_mul",
    "tensor_max", "tensor_scalar", "tensor_scalar_add", "tensor_scalar_sub",
    "tensor_scalar_mul", "tensor_scalar_max", "tensor_scalar_min",
    "tensor_single_scalar", "tensor_reduce", "reduce_max", "reduce_sum",
    "scalar_tensor_tensor", "tensor_tensor_scan", "memset", "memzero",
    "reciprocal", "select", "stream_shuffle", "transpose", "activation",
    "copy", "mul", "add", "square", "sqrt", "sign", "iota", "tensor_relu",
}


class EngProxy:

    def __init__(self, B, eng, name, sem, nosync=False):
        self._B = B
        self._eng = eng
        self._name = name
        self._sem = sem
        self._nosync = nosync

    def __getattr__(self, k):
        f = getattr(self._eng, k)
        if k not in _DATA_OPS:
            return f
        B, eng, sem, name = self._B, self._eng, self._sem, self._name
        nosync = self._nosync

        def wrap(*a, **kw):
            ins = f(*a, **kw)
            ins.then_inc(sem.h, 1)
            sem.n += 1
            if not nosync:
                eng.wait_ge(sem.h, sem.n)
                B._waited.setdefault(name, {})[sem] = sem.n
            B._auto_evs[id(ins)] = (sem, sem.n)
            B._auto_refs.append(ins)
            return ins
        return wrap


class _SubBlock:

    def __init__(self, blk):
        self.blk = blk

    def __enter__(self):
        return self.blk

    def __exit__(self, *a):
        return False


def _maybe_block(nc, blk):
    return nc.Block() if blk is None else _SubBlock(blk)


def dram3(ap_like, offset, dims):
    return bass.AP(ap_like.tensor, offset, dims)


class WStream:

    def __init__(self, B):
        self.B = B
        self.idx = 0
        self.done_ev = {}
        self.load_ev = {}

    def issue(self, g, dmas, engname="gpsimd", sems=None):
        B = self.B
        prev = self.idx - 2
        if prev >= 0:
            B.w(g, engname, self.done_ev[prev], future=True)
        sem = (sems or B.sWp)[self.idx % 2]
        for dst, src in dmas:
            B.ev(sem, g.dma_start(out=dst, in_=src), 16)
        self.load_ev[self.idx] = B.mark(sem)
        gi = self.idx
        self.idx += 1
        return gi, gi % 2

    def done(self, gi, event):
        self.done_ev.setdefault(gi, event)


def build(debug=False):
    B = KB(debug)
    B.ws = WStream(B)
    B.final_done = None
    B.park_done = None
    phase_setup(B)
    for h in range(2):
        if debug:
            phase_p0(B, h)
            phase_scan(B, h)
            for l in range(NL):
                phase_qkv(B, h, l)
                if l == 0:
                    phase_dbg_qkv(B, h)
                phase_attn(B, h, l)
                if l == 0:
                    phase_dbg_o(B, h)
                phase_wo(B, h, l)
                phase_ln(B, h, l)
                phase_ffn(B, h, l)
            phase_dbg_u(B, h, "dbg_u1")
            phase_final(B, h)
        else:
            with B.nc.Block() as b1:
                phase_p0(B, h, b1)
                phase_scan(B, h, b1)
                for l in range(NL):
                    phase_qkv(B, h, l, b1)
                    phase_attn(B, h, l, b1)
                    phase_wo(B, h, l, b1)
                    phase_ln(B, h, l, b1)
                    phase_ffn(B, h, l, b1)
                phase_final(B, h, b1)

    with B.nc.Block() as blk:
        @blk.sync
        def _(s):
            B.w(s, "sync", B.mark(B.sP), B.mark(B.sPp[0]), B.mark(B.sPp[1]))
    B.stack.close()
    return B


def phase_setup(B):
    with B.nc.Block() as blk:
        @blk.sync
        def _(s):
            for nm, tile in [("c_ident32", B.ident32), ("c_identb", B.identb),
                             ("c_stair", B.stair), ("c_triag", B.triag),
                             ("c_selsq", B.selsq), ("c_onesb", B.onesb),
                             ("c_stairb", B.stairb), ("c_triagb", B.triagb),
                             ("c_onesr", B.onesr), ("c_selB", B.selB),
                             ("c_jmat", B.jmat), ("c_jmatb", B.jmatb),
                             ("c_selB2", B.selB2), ("c_scal", B.scal)]:
                B.ev(B.sX, s.dma_start(out=tile[:], in_=getattr(B, "d" + nm)), 16)
            for dsrc, dst in [(B.d_a, B.acol), (B.d_b, B.bcol),
                              (B.d_bout, B.boutc)]:
                B.ev(B.sX, s.dma_start(
                    out=dst[:], in_=dram3(dsrc, 0, [[1, 128], [128, 8]]),
                    allow_slow_non_contiguous=True), 16)
        ld = B.mark(B.sX)

        @blk.scalar
        def _(sc):

            sc = B.S
            B.w(sc, "scalar", ld)
            B.ev(B.sS, sc.activation(B.ah8[:], B.acol[:], AF.Tanh))

        @blk.vector
        def _(v):

            v = B.V
            B.ev(B.sV, v.memset(B.carry[:], 0.0))


def emit_precast(B, g, l, chunks):
    CH = 43680
    for c in chunks:
        if c < 4:
            n = W1PN // 4
            src_t, src_o = B.d_w1p.tensor, l * W1PN + c * n
            dst_o = l * LAYER_PN + c * n
        else:
            n = W2PN // 2
            src_t, src_o = B.d_w2p.tensor, l * W2PN + (c - 4) * n
            dst_o = l * LAYER_PN + W1PN + (c - 4) * n
        B.ev(B.sPC[l], g.dma_start(
            out=bass.AP(B.d_wpre.tensor, dst_o, [[CH, n // CH], [1, CH]]),
            in_=bass.AP(src_t, src_o, [[CH, n // CH], [1, CH]])), 16)


def phase_p0(B, h, blk_=None):
    nc = B.nc
    t0 = h * H
    XT = [0, 1024]
    XSQ = 2048
    XN = [3072, None]
    C = 4200

    def stats(par):
        c = C + 8 * par
        return (B.un(c + 0, 1), B.un(c + 1, 1), B.un(c + 2, 1),
                B.un(c + 4, 1), B.un(c + 5, 1), B.un(c + 6, 1),
                B.un(c + 7, 1))
    TLO = 4300

    with _maybe_block(nc, blk_) as blk:
        x_ev = [None] * 8
        tp_done = [None] * 8
        u_evac = [None] * 8
        sums_ev = [None] * 8
        sb = B.sS.n
        xn_done = [(B.sS, sb + 4 * (tb + 1)) for tb in range(8)]

        def xnview(tb):
            if tb % 2 == 0:
                return B.un(XN[0], 1024)
            return B.probs[:, 0:2048].bitcast(F32)[:, 0:1024]

        def issue_x(s, tb):
            sem = B.sXp[tb % 2]
            B.ev(sem, s.dma_start(
                out=B.un(XT[tb % 2], 1024),
                in_=B.d_x[t0 + tb * 128:t0 + (tb + 1) * 128, :]), 16)
            x_ev[tb] = B.mark(sem)

        @blk.sync
        def _(s):
            if B.final_done is not None:
                B.w(s, "sync", *B.final_done)
            issue_x(s, 0)
            issue_x(s, 1)

        if h == 0 and B.sPW.n == 0:
            @blk.gpsimd
            def _(g):
                for c in range(2):
                    n = D * D // 2
                    B.ev(B.sPW, g.dma_start(
                        out=bass.AP(B.d_wpre.tensor, B.pre_off["wout"] + c * n,
                                    [[65536, n // 65536], [1, 65536]]),
                        in_=bass.AP(B.d_woutp.tensor, c * n,
                                    [[65536, n // 65536], [1, 65536]])), 16)

        def emit_stats(tb):
            @blk.vector
            def _(v, tb=tb):
                v = B.V
                if B.final_done is not None:
                    B.w(v, "vector", *B.final_done)
                B.w(v, "vector", x_ev[tb])
                if tb >= 2:
                    B.wf(v, "vector", xn_done[tb - 2])
                m_neg, msq_n, var, r_col, bias_c, s1, s2 = stats(tb % 2)
                v.tensor_mul(B.un(XSQ, 1024), B.un(XT[tb % 2], 1024),
                             B.un(XT[tb % 2], 1024))
                v.reduce_sum(s1, B.un(XT[tb % 2], 1024), axis=AXX)
                v.reduce_sum(s2, B.un(XSQ, 1024), axis=AXX)
                v.tensor_scalar_mul(m_neg, s1, -1.0 / D)
                v.scalar_tensor_tensor(msq_n, m_neg, -1.0, m_neg,
                                       ALU.mult, ALU.mult)
                sums_ev[tb] = B.ev(B.sV, v.scalar_tensor_tensor(
                    var, s2, 1.0 / D, msq_n, ALU.mult, ALU.add))

        def emit_tp(tb):
            @blk.tensor
            def _(t, tb=tb):
                B.wf(t, "tensor", xn_done[tb])
                if tb >= 2:
                    B.w(t, "tensor", u_evac[tb - 2])
                reg = B.psb(0 if tb % 2 == 0 else 4, 1024)
                xv = xnview(tb)
                for g in range(NG):
                    ins = t.transpose(reg[:, g * 128:(g + 1) * 128],
                                      xv[:, g * 128:(g + 1) * 128],
                                      B.ident32[:])
                tp_done[tb] = B.ev(B.sT, ins)

            @blk.vector
            def _(v, tb=tb):
                v = B.V
                B.w(v, "vector", tp_done[tb])
                reg = B.psb(0 if tb % 2 == 0 else 4, 1024)
                for g in range(NG):
                    vv = B.Vn if g < NG - 1 else B.V
                    ins = vv.tensor_copy(
                        B.u[g][:, 1 + tb * 128:1 + (tb + 1) * 128],
                        reg[:, g * 128:(g + 1) * 128])
                u_evac[tb] = B.ev(B.sV, ins)

        for tb in range(8):
            emit_stats(tb)
            if tb + 2 < 8:
                @blk.sync
                def _(s, tb=tb):
                    B.wf(s, "sync", xn_done[tb])
                    issue_x(s, tb + 2)
            if tb >= 1:
                emit_tp(tb - 1)
        emit_tp(7)

        @blk.scalar
        def _(sc):
            sc = B.S
            for tb in range(8):
                m_neg, msq_n, var, r_col, bias_c, s1, s2 = stats(tb % 2)
                B.w(sc, "scalar", sums_ev[tb])
                if tb >= 2:
                    B.w(sc, "scalar", tp_done[tb - 2])
                sc.activation(var, var, AF.Ln, bias=1e-5)
                sc.activation(r_col, var, AF.Exp, scale=-0.5)
                sc.activation(bias_c, m_neg, AF.Identity,
                              scale=r_col[:, 0:1])
                ev = B.ev(B.sS, sc.activation(
                    xnview(tb), B.un(XT[tb % 2], 1024), AF.Identity,
                    bias=bias_c[:, 0:1], scale=r_col[:, 0:1]))
                assert ev == xn_done[tb], (ev, xn_done[tb])

        B.tabadd_ev = [None] * 4

        @blk.sync
        def _(s):
            if B.final_done is not None:
                B.w(s, "sync", *B.final_done)
            for k in range(4):
                B.ev(B.sX, s.dma_start(
                    out=B.Bb[2 * k][:, 0:H],
                    in_=bass.AP(B.dc_tabhi.tensor, k * 128 * T + t0,
                                [[T, 128], [1, H]])), 16)
                B.ev(B.sX, s.dma_start(
                    out=B.Bb[2 * k + 1][:, 0:H],
                    in_=bass.AP(B.dc_tablo.tensor, k * 128 * T + t0,
                                [[T, 128], [1, H]])), 16)
            B.tabld_ev = B.mark(B.sX)

        @blk.vector
        def _(v):

            v = B.V
            B.w(v, "vector", B.tabld_ev)
            for k in range(4):
                v.tensor_add(B.Bb[2 * k][:, 0:H], B.Bb[2 * k][:, 0:H],
                             B.Bb[2 * k + 1][:, 0:H])
                B.tabadd_ev[k] = B.ev(B.sV, v.tensor_copy(
                    B.tabsb[:, k * H:(k + 1) * H], B.Bb[2 * k][:, 0:H]))

        @blk.vector
        def _(v):

            v = B.V
            for g in range(NG):
                ins = v.memset(B.u[g][:, 0:1], 0.0)
            B.ev(B.sV, ins)


def phase_scan(B, h, blk_=None):
    nc = B.nc
    t0 = h * H
    YB = 0
    WOL = [4096, 4608]
    PKS = 5120

    with _maybe_block(nc, blk_) as blk:
        ycast_ev = [None] * NG
        for g in range(NG):
            @blk.vector
            def _(v):

                v = B.V
                e = B.ev(B.sV, v.tensor_scalar_mul(
                    B.un(PKS, H), B.u[g][:, 1:H + 1], B.bcol[:, g:g + 1]))
                B.w(v, "vector", e)
                init = 0.0 if h == 0 else B.carry[:, g:g + 1]
                d0 = B.ah8[:, g:g + 1].broadcast_to([128, H])
                e = B.ev(B.sV, v.tensor_tensor_scan(
                    B.Bb[g][:, 0:H], d0, B.un(PKS, H), init, ALU.mult, ALU.add))
                if h == 0:
                    B.w(v, "vector", e)
                    v.tensor_copy(B.carry[:, g:g + 1], B.Bb[g][:, H - 1:H])
                ycast_ev[g] = B.ev(B.sV, v.tensor_copy(
                    B.unb(YB + g * 512, H), B.Bb[g][:, 0:H]))

        ev_dma = [None] * 8
        ev_mm = [None] * 8
        ev_evac = [None] * 8

        def issue_wout(s, ec):
            if ec >= 2:
                B.w(s, "sync", ev_mm[ec - 2])
            elif h == 0:
                B.w(s, "sync", (B.sPW, B.sPW.n))
            sem = B.sDp[ec % 2]
            B.ev(sem, s.dma_start(
                out=B.wsb(ec % 2)[:, 0:1024],
                in_=bass.AP(B.d_wpre.tensor, B.pre_off["wout"] + ec * 1024 * 128,
                            [[1024, 128], [1, 1024]])), 16)
            ev_dma[ec] = B.mark(sem)

        @blk.sync
        def _(s):
            issue_wout(s, 0)
            issue_wout(s, 1)

        for ec in range(8):
            @blk.tensor
            def _(t):
                B.w(t, "tensor", ev_dma[ec], *ycast_ev)
                if ec >= 2:
                    B.w(t, "tensor", ev_evac[ec - 2])
                reg = B.psb(0 if ec % 2 == 0 else 2, 1024)
                for kt in range(NG):
                    for tc in range(2):
                        ins = t.matmul(
                            reg[:, tc * 512:(tc + 1) * 512],
                            B.wsb(ec % 2)[:, kt * 128:(kt + 1) * 128],
                            B.unb(YB + kt * 512, H)[:, tc * 512:(tc + 1) * 512],
                            start=(kt == 0), stop=(kt == NG - 1))
                ev_mm[ec] = B.ev(B.sT, ins)

            @blk.scalar
            def _(sc):

                sc = B.S
                B.w(sc, "scalar", ev_mm[ec])
                reg = B.psb(0 if ec % 2 == 0 else 2, 1024)
                ev_evac[ec] = B.ev(B.sS, sc.activation(
                    B.parkSB[ec][:], reg, AF.Identity,
                    bias=B.boutc[:, ec:ec + 1]))

            B.scan_evac = ev_evac
            B.scan_mm = ev_mm

            if ec + 2 < 8:
                @blk.sync
                def _(s):
                    issue_wout(s, ec + 2)


def _load_cols_layer(B, s, l):
    B.ev(B.sX, s.dma_start(out=B.qs8[0:64, 0:1],
                           in_=dram3(B.d_qs, l * DH, [[1, 64], [1, 1]])), 16)
    B.ev(B.sX, s.dma_start(out=B.qs8[64:128, 0:1],
                           in_=dram3(B.d_qs, l * DH, [[1, 64], [1, 1]])), 16)
    B.ev(B.sX, s.dma_start(out=B.ks8[0:64, 0:1],
                           in_=dram3(B.d_ks, l * DH, [[1, 64], [1, 1]])), 16)
    B.ev(B.sX, s.dma_start(out=B.ks8[64:128, 0:1],
                           in_=dram3(B.d_ks, l * DH, [[1, 64], [1, 1]])), 16)
    B.ev(B.sX, s.dma_start(out=B.lgc[:],
                           in_=dram3(B.d_lng, l * D, [[1, 128], [128, 8]]),
                           allow_slow_non_contiguous=True), 16)
    B.ev(B.sX, s.dma_start(out=B.lbc[:],
                           in_=dram3(B.d_lnb, l * D, [[1, 128], [128, 8]]),
                           allow_slow_non_contiguous=True), 16)
    B.ev(B.sX, s.dma_start(out=B.b2c[:],
                           in_=dram3(B.d_b2, l * D, [[1, 128], [128, 8]]),
                           allow_slow_non_contiguous=True), 16)
    B.ev(B.sX, s.dma_start(out=B.b1a[:, 0:21],
                           in_=dram3(B.d_b1, l * W1N, [[1, 128], [128, 21]]),
                           allow_slow_non_contiguous=True), 16)
    B.ev(B.sX, s.dma_start(out=B.b1a[0:42, 21:22],
                           in_=dram3(B.d_b1, l * W1N + 21 * 128,
                                     [[1, 42], [1, 1]])), 16)
    B.ev(B.sX, s.dma_start(out=B.b1g[:, 0:21],
                           in_=dram3(B.d_b1, l * W1N + HID, [[1, 128], [128, 21]]),
                           allow_slow_non_contiguous=True), 16)
    B.ev(B.sX, s.dma_start(out=B.b1g[0:42, 21:22],
                           in_=dram3(B.d_b1, l * W1N + HID + 21 * 128,
                                     [[1, 42], [1, 1]])), 16)
    return B.mark(B.sX)


def phase_qkv(B, h, l, blk_=None):
    nc = B.nc
    SQB = [4096, 4608]
    ROTB, QHB = 5120, 5632

    def reg_of(slot):
        return [B.psA[:, 0:1024], B.psA[:, 1024:2048], B.psB[:, 0:1024]][slot]

    def rawb(c):
        return B.unb((c % 4) * 512 + (0 if c < 4 else 2048), 1024)

    with _maybe_block(nc, blk_) as blk:
        mm_ev = [None] * 12
        cons_ev = [None] * 12
        qraw_ev = [None] * 12
        n2done = [None, None]
        rows_ev = [None, None]
        bc_ev = [None] * 8
        apply_ev = [None] * 8
        rotmm_ev = [None] * 8
        rotev_ev = [None] * 8
        t2_ev = [None] * 8
        B.qprime_ev = [None] * 4
        B.kprime_ev = [None] * 4
        vtp_ev = [None] * 4
        B.vev_ev = [None] * 4
        grp_of = {}
        ld_of = {}

        sb = B.sS.n
        B.qs8_ev = (B.sS, sb + 1)
        qevac_pred = [None] * 8
        for c in range(4):
            qevac_pred[c] = (B.sS, sb + 2 + c)
            qevac_pred[4 + c] = (B.sS, sb + 8 + c)
        sqrt_pred = [(B.sS, sb + 7), (B.sS, sb + 13)]
        vevac_pred = [(B.sS, sb + 14 + c) for c in range(4)]
        rot_pred = [(B.sS, sb + 18 + 9 * g_) for g_ in range(4)] + \
            [(B.sS, sb + 54 + i) for i in range(4)]
        vev_pred = [(B.sS, sb + 26 + 9 * g_) for g_ in range(4)]
        B.vev_ev = vev_pred
        B.vevac_pred = vevac_pred
        B.rot_pred = rot_pred

        @blk.sync
        def _(s):
            if l == 1:
                B.w(s, "sync", *B.ffn_done_evs)
            B.lcols_ev = _load_cols_layer(B, s, l)

        @blk.vector
        def _(v):
            v = B.V
            if h == 0:
                for g in range(NG):
                    vv = B.Vn if g < NG - 1 else B.V
                    ins = vv.tensor_copy(B.hQ[l][:, g:g + 1],
                                         B.u[g][:, H:H + 1])
                B.ev(B.sV, ins)
            else:
                for g in range(NG):
                    vv = B.Vn if g < NG - 1 else B.V
                    ins = vv.tensor_copy(B.u[g][:, 0:1], B.hQ[l][:, g:g + 1])
                B.ev(B.sV, ins)
            stg_ev = []
            for g in range(NG):
                off = 1 if g < 4 else 0
                vv = B.Vn if g < NG - 1 else B.V
                stg_ev.append(B.ev(B.sV, vv.tensor_copy(
                    B.stg[g][:], B.u[g][:, off:off + H])))
            B.stg_ev = stg_ev

        def issue_qkv_group(g, grp):
            if l == 0 and grp < 2:
                B.w(g, "gpsimd", B.scan_mm[6 + grp])
            if l == 1 and grp < 2:
                B.w(g, "gpsimd", B.w2mm_pred[7 - grp])
            c0 = grp * 2
            gi, par = B.ws.issue(g, [(
                B.wsb(B.ws.idx % 2)[:, 0:2048],
                bass.AP(B.d_wqkv.tensor, l * D * 3 * INNER + c0 * 128,
                        [[3 * INNER, 128], [128 * 3 * INNER, 8], [1, 256]]))])
            for c in range(2):
                grp_of[c0 + c] = (gi, par)
                ld_of[c0 + c] = B.ws.load_ev[gi]

        @blk.gpsimd
        def _(g):
            g = B.G
            issue_qkv_group(g, 0)
            issue_qkv_group(g, 1)

        def lhsT(c, kt):
            gi, par = grp_of[c]
            off = kt * 256 + (c % 2) * 128
            return B.wsb(par)[:, off:off + 128]

        def emit_chunk_mm(c):
            @blk.tensor
            def _(t, c=c):
                slot = c % 3
                B.w(t, "tensor", ld_of[c], *B.stg_ev)
                if c < 2 and l == 0:
                    B.w(t, "tensor", B.scan_evac[6 + c])
                if c >= 3:
                    B.wf(t, "tensor", cons_ev[c - 3])
                reg = reg_of(slot)
                for kt in range(NG):
                    for tc in range(2):
                        ins = t.matmul(reg[:, tc * 512:(tc + 1) * 512],
                                       lhsT(c, kt),
                                       B.stg[kt][:, tc * 512:(tc + 1) * 512],
                                       start=(kt == 0), stop=(kt == NG - 1))
                mm_ev[c] = B.ev(B.sT, ins)
                if c % 2 == 1:
                    B.ws.done(grp_of[c][0], mm_ev[c])

            if c % 2 == 1 and c + 3 < 12:
                @blk.gpsimd
                def _(g, c=c):
                    g = B.G
                    issue_qkv_group(g, (c + 3) // 2)

        for c in range(8):
            emit_chunk_mm(c)
            qk = 0 if c < 4 else 1
            g_ = c % 4

            @blk.vector
            def _(v, c=c):
                v = B.V
                B.wf(v, "vector", qevac_pred[c])
                if c >= 2:
                    B.w(v, "vector", cons_ev[c - 2])
                qraw_ev[c] = B.ev(B.sV, v.tensor_mul(
                    B.unb(SQB[c % 2], 1024), rawb(c), rawb(c)))

            @blk.tensor
            def _(t, c=c, qk=qk, g_=g_):
                B.w(t, "tensor", qraw_ev[c])
                p0 = 0 if qk == 0 else 32
                n2 = B.psB[p0:p0 + 8, 1024:2048]
                for tc in range(2):
                    ins = t.matmul(n2[:, tc * 512:(tc + 1) * 512],
                                   B.selsq[:, g_ * 8:(g_ + 1) * 8],
                                   B.unb(SQB[c % 2], 1024)[:, tc * 512:(tc + 1) * 512],
                                   start=(g_ == 0), stop=(g_ == 3),
                                   tile_position=(0, p0))
                cons_ev[c] = B.ev(B.sT, ins)
                if g_ == 3:
                    n2done[qk] = cons_ev[c]

        def vdm(g_):
            return B.probs[:, g_ * 1024:(g_ + 1) * 1024]

        for c in range(8, 12):
            cons_ev[c] = vevac_pred[c - 8]
            emit_chunk_mm(c)

        for qk in range(2):
            @blk.vector
            def _(v, qk=qk):
                v = B.V
                B.wf(v, "vector", sqrt_pred[qk])
                dst = B.rq[:] if qk == 0 else B.rk40[32:40, :]
                e = B.ev(B.sV, v.tensor_scalar_max(dst, dst, 1e-12))
                B.w(v, "vector", e)
                rows_ev[qk] = B.ev(B.sV, v.reciprocal(dst, dst))

        def emit_vtp(g_):
            @blk.tensor
            def _(t, g_=g_):
                B.wf(t, "tensor", cons_ev[8 + g_], cons_ev[11])
                if g_ >= 1:
                    B.wf(t, "tensor", B.vev_ev[g_ - 1])
                for tb in range(8):
                    ins = t.transpose(
                        B.psB[:, tb * 64:(tb + 1) * 64].bitcast(BF16),
                        vdm(g_)[:, tb * 128:(tb + 1) * 128],
                        B.identb[:])
                vtp_ev[g_] = B.ev(B.sT, ins)

        for qk in range(2):
            for g_ in range(4):
                c = qk * 4 + g_

                @blk.tensor
                def _(t, qk=qk, g_=g_, c=c):
                    B.w(t, "tensor", rows_ev[0], rows_ev[1],
                        n2done[0], n2done[1])
                    if c > 0 and apply_ev[c - 1] is not None:
                        B.w(t, "tensor", apply_ev[c - 1])
                    bc = B.psB[:, 1024:2048]
                    p0 = 0 if qk == 0 else 32
                    lhs_ = (B.selB[:, g_ * 128:(g_ + 1) * 128] if qk == 0
                            else B.selB2[32:40, g_ * 128:(g_ + 1) * 128])
                    srcr = (B.rq[:, :] if qk == 0 else B.rk40[32:40, :])
                    for tc in range(2):
                        ins = t.matmul(bc[:, tc * 512:(tc + 1) * 512],
                                       lhs_, srcr[:, tc * 512:(tc + 1) * 512],
                                       start=True, stop=True,
                                       tile_position=(p0, 0))
                    bc_ev[c] = B.ev(B.sT, ins)

                @blk.vector
                def _(v, qk=qk, g_=g_, c=c):
                    v = B.V
                    B.wf(v, "vector", B.qs8_ev)
                    B.w(v, "vector", bc_ev[c], qraw_ev[c])
                    bc = B.psB[:, 1024:2048]
                    scal = B.qs8 if qk == 0 else B.ks8
                    apply_ev[c] = B.ev(B.sV, v.scalar_tensor_tensor(
                        B.unb(QHB, 1024), rawb(c), scal[:, 0:1], bc,
                        ALU.mult, ALU.mult))

                @blk.tensor
                def _(t, c=c):
                    B.w(t, "tensor", apply_ev[c])
                    B.wf(t, "tensor", cons_ev[9] if c % 2 == 0 else cons_ev[10])
                    if c >= 2:
                        B.wf(t, "tensor", rotev_ev[c - 2])
                    pb = (c % 2) * 1024
                    for tc in range(2):
                        ins = t.matmul(
                            B.psA[:, pb + tc * 512:pb + (tc + 1) * 512],
                            B.jmatb[:], B.unb(QHB, 1024)[:, tc * 512:(tc + 1) * 512],
                            start=True, stop=True)
                    rotmm_ev[c] = B.ev(B.sT, ins)

                rotev_ev[c] = rot_pred[c]

                @blk.vector
                def _(v, qk=qk, g_=g_, c=c):
                    v = B.V
                    B.w(v, "vector", apply_ev[c], *B.tabadd_ev)
                    B.w(v, "vector", rotmm_ev[c])
                    tcb = B.tabsb[:, (0 if qk == 0 else 2 * H):]
                    tsb = B.tabsb[:, (H if qk == 0 else 3 * H):]
                    v.tensor_mul(B.unb(QHB, 1024), B.unb(QHB, 1024),
                                 tcb[:, 0:H])
                    B.wf(v, "vector", rotev_ev[c])
                    t2_ev[c] = B.ev(B.sV, v.tensor_mul(
                        B.unb(ROTB, 1024), B.unb(ROTB, 1024), tsb[:, 0:H]))
                    dstf = B.qp(g_) if qk == 0 else B.kp(g_)
                    e = B.ev(B.sV, v.tensor_add(dstf, B.unb(QHB, 1024),
                                                B.unb(ROTB, 1024)))
                    if qk == 0:
                        B.qprime_ev[g_] = e
                    else:
                        B.kprime_ev[g_] = e
                        if h == 0:
                            B.w(v, "vector", e)
                            B.ev(B.sV, v.tensor_copy(
                                B.halo_k[l][:, g_ * 512:(g_ + 1) * 512],
                                B.kp(g_)[:, 512:1024]))

                if qk == 0:
                    emit_vtp(g_)

        if h == 0:
            @blk.vector
            def _(v):
                v = B.V
                B.wf(v, "vector", B.vev_ev[3])
                for tb in range(4, 8):
                    ins = v.tensor_copy(
                        B.halo_v[l][:, (tb - 4) * 512:(tb - 3) * 512],
                        B.vtm(tb))
                B.ev(B.sV, ins)

        @blk.scalar
        def _(sc):
            sc = B.S
            B.w(sc, "scalar", B.lcols_ev)
            ev = B.ev(B.sS, sc.mul(B.qs8[:], B.qs8[:], 8.0))
            assert ev == B.qs8_ev, (ev, B.qs8_ev)
            for qk in range(2):
                for c in range(4 * qk, 4 * qk + 4):
                    B.w(sc, "scalar", mm_ev[c])
                    ev = B.ev(B.sS, sc.activation(
                        rawb(c), reg_of(c % 3), AF.Identity))
                    assert ev == qevac_pred[c], (ev, qevac_pred[c])
                B.w(sc, "scalar", n2done[qk])
                dst = B.rq[:] if qk == 0 else B.rk40[32:40, :]
                p0 = 0 if qk == 0 else 32
                n2 = B.psB[p0:p0 + 8, 1024:2048]
                B.ev(B.sS, sc.activation(dst, n2, AF.Ln, bias=1e-30))
                ev = B.ev(B.sS, sc.activation(dst, dst, AF.Exp, scale=0.5))
                assert ev == sqrt_pred[qk], (ev, sqrt_pred[qk])
            for c in range(8, 12):
                B.w(sc, "scalar", mm_ev[c])
                ev = B.ev(B.sS, sc.activation(vdm(c - 8), reg_of(c % 3),
                                              AF.Identity))
                assert ev == vevac_pred[c - 8], (ev, vevac_pred[c - 8])
            for c in range(8):
                B.w(sc, "scalar", rotmm_ev[c])
                pb = (c % 2) * 1024
                ev = B.ev(B.sS, sc.activation(
                    B.unb(ROTB, 1024), B.psA[:, pb:pb + 1024], AF.Identity))
                assert ev == rot_pred[c], (ev, rot_pred[c])
                if c < 4:
                    B.w(sc, "scalar", vtp_ev[c])
                    for tb in range(8):
                        ss = B.Sn if tb < 7 else B.S
                        ins = ss.activation(
                            B.vtm(tb)[:, c * 128:(c + 1) * 128],
                            B.psB[:, tb * 64:(tb + 1) * 64].bitcast(BF16),
                            AF.Identity)
                    ev = B.ev(B.sS, ins)
                    assert ev == vev_pred[c], (ev, vev_pred[c])



def phase_attn(B, h, l, blk_=None):
    nc = B.nc

    with _maybe_block(nc, blk_) as blk:
        if h == 0 and B.sPC[l].n == 0:
            @blk.gpsimd
            def _(g):
                emit_precast(B, g, l, range(6))

        prev = {"exp": [None] * NH, "tev": [None] * NH, "oev": [None] * 4}
        for qc in range(8):
            gq = h * 8 + qc
            nkt = min(gq, 4) + 1
            w = nkt * 128
            kt_list = list(range(gq - nkt + 1, gq + 1))

            def simreg(hh):
                return (B.psA[:, 0:640] if hh % 2 == 0
                        else B.psA[:, 1024:1664])

            sb0 = B.sS.n
            exp_pred = [None] * NH
            tev_pred = [None] * NH
            for hh in range(NH):
                exp_pred[hh] = (B.sS, sb0 + hh + 1)
            nv = B.sV.n
            norm_pred = [None] * NH
            oev_pred = [None] * 4
            for pj in range(4):
                norm_pred[2 * pj] = (B.sV, nv + 3 * pj + 2)
                norm_pred[2 * pj + 1] = (B.sV, nv + 3 * pj + 3)
            nv += 12
            for pj in range(4):
                if pj >= 1:
                    oev_pred[pj - 1] = (B.sV, nv + 2)
                    nv += 2
                tev_pred[2 * pj] = (B.sV, nv + 1)
                tev_pred[2 * pj + 1] = (B.sV, nv + 2)
                nv += 2
            oev_pred[3] = (B.sV, nv + 2)
            sim_pred = [(B.sT, B.sT.n + hh + 1) for hh in range(NH)]
            ntt = B.sT.n + NH
            tp_pred = [None] * NH
            av_pred = [None] * 4
            for pj in range(4):
                tp_pred[2 * pj] = (B.sT, ntt + 1)
                tp_pred[2 * pj + 1] = (B.sT, ntt + 2)
                av_pred[pj] = (B.sT, ntt + 3)
                ntt += 3

            pv = prev

            @blk.tensor
            def _(t, qc=qc, gq=gq, nkt=nkt, w=w, kt_list=kt_list,
                  exp_pred=exp_pred, norm_pred=norm_pred, tev_pred=tev_pred,
                  oev_pred=oev_pred, sim_pred=sim_pred, tp_pred=tp_pred,
                  av_pred=av_pred, pv=pv, simreg=simreg):
                if qc == 0:
                    B.wf(t, "tensor", B.vevac_pred[1], B.vevac_pred[2],
                         B.vevac_pred[3], B.rot_pred[6], B.rot_pred[7])
                for hh in range(NH):
                    g_ = hh // 2
                    sub = hh % 2
                    B.w(t, "tensor", B.qprime_ev[g_], B.kprime_ev[g_])
                    if hh >= 2:
                        B.wf(t, "tensor", exp_pred[hh - 2])
                    elif pv["exp"][hh + 6] is not None:
                        B.w(t, "tensor", pv["exp"][hh + 6])
                    reg = simreg(hh)
                    lhs = B.qp(g_)[sub * 64:sub * 64 + 64,
                                   qc * 128:(qc + 1) * 128]
                    runs = []
                    for j in range(nkt):
                        tbk = kt_list[j]
                        if tbk >= h * 8:
                            srcid, s0 = 0, (tbk - h * 8) * 128
                        else:
                            srcid, s0 = 1, g_ * 512 + (tbk - 4) * 128
                        if runs and runs[-1][0] == srcid and \
                                runs[-1][2] + runs[-1][3] == s0:
                            runs[-1][3] += 128
                        else:
                            runs.append([srcid, j * 128, s0, 128])
                    bank_ops = {}
                    for srcid, col, s0, n in runs:
                        while n > 0:
                            seg = min(n, 512 - col % 512) if col % 512 else \
                                min(n, 512)
                            base = (B.kp(g_) if srcid == 0 else B.halo_k[l])
                            rhs = base[sub * 64:sub * 64 + 64, s0:s0 + seg]
                            bank_ops.setdefault(col // 512, []).append(
                                (col, lhs, rhs, (sub * 64, 0)))
                            col += seg
                            s0 += seg
                            n -= seg
                    if gq >= 4:
                        bank_ops[0].append((0, B.identb[:], B.stairb[:], None))
                    bank_ops[(w - 128) // 512].append(
                        (w - 128, B.identb[:], B.triagb[:], None))
                    for bk in sorted(bank_ops):
                        ops = bank_ops[bk]
                        for oi, (col, lh, rh, tpos) in enumerate(ops):
                            kw = {}
                            if tpos is not None:
                                kw["tile_position"] = tpos
                            ncol = rh.shape[-1] if tpos == (sub * 64, 0) \
                                else 128
                            ins = t.matmul(reg[:, col:col + ncol], lh, rh,
                                           start=(oi == 0),
                                           stop=(oi == len(ops) - 1), **kw)
                    ev = B.ev(B.sT, ins)
                    assert ev == sim_pred[hh], (ev, sim_pred[hh])
                for pj in range(4):
                    for hh in (2 * pj, 2 * pj + 1):
                        B.wf(t, "tensor", norm_pred[hh])
                        c0 = (hh % 2) * 512
                        if hh >= 2:
                            B.wf(t, "tensor", tev_pred[hh - 2])
                        elif pv["tev"][hh + 6] is not None:
                            B.w(t, "tensor", pv["tev"][hh + 6])
                        for j in range(nkt):
                            ins = t.transpose(
                                B.psB[:, c0 + j * 64:c0 + (j + 1) * 64].bitcast(BF16),
                                B.probs[:, hh * 640 + j * 128:
                                         hh * 640 + (j + 1) * 128],
                                B.identb[:])
                        ev = B.ev(B.sT, ins)
                        assert ev == tp_pred[hh], (ev, tp_pred[hh])
                    B.wf(t, "tensor", tev_pred[2 * pj], tev_pred[2 * pj + 1])
                    if qc == 0:
                        B.wf(t, "tensor", B.vev_ev[3])
                    if pj >= 1:
                        B.wf(t, "tensor", oev_pred[pj - 1])
                    elif pv["oev"][3] is not None:
                        B.w(t, "tensor", pv["oev"][3])
                    for j in range(nkt):
                        tbk = kt_list[j]
                        for sub in range(2):
                            hh = 2 * pj + sub
                            av = B.psB[sub * 64:sub * 64 + 64,
                                       1024 + sub * 512:1024 + sub * 512 + 128]
                            if tbk >= h * 8:
                                vs = B.vtm(tbk - h * 8)[:, hh * 64:(hh + 1) * 64]
                            else:
                                vs = B.halo_v[l][:, (tbk - 4) * 512 + hh * 64:
                                                 (tbk - 4) * 512 + (hh + 1) * 64]
                            ins = t.matmul(av, vs,
                                           B.attnT[hh % 2][:, j * 128:(j + 1) * 128],
                                           start=(j == 0), stop=(j == nkt - 1),
                                           tile_position=(0, sub * 64))
                    ev = B.ev(B.sT, ins)
                    assert ev == av_pred[pj], (ev, av_pred[pj])

            @blk.vector
            def _(v, qc=qc, gq=gq, nkt=nkt, w=w, exp_pred=exp_pred,
                  norm_pred=norm_pred, tev_pred=tev_pred,
                  oev_pred=oev_pred, sim_pred=sim_pred, tp_pred=tp_pred,
                  av_pred=av_pred, pv=pv, simreg=simreg):
                v = B.V
                for pj in range(4):
                    B.wf(v, "vector", exp_pred[2 * pj], exp_pred[2 * pj + 1])
                    rc = B.ev(B.sV, v.reciprocal(
                        B.rcp[:, 2 * pj:2 * pj + 2],
                        B.rsum[:, 2 * pj:2 * pj + 2]))
                    B.w(v, "vector", rc)
                    for hh in (2 * pj, 2 * pj + 1):
                        vv = B.Vn if hh % 2 == 0 else B.V
                        ev = B.ev(B.sV, vv.tensor_scalar_mul(
                            B.probs[:, hh * 640:hh * 640 + w],
                            B.probs[:, hh * 640:hh * 640 + w],
                            B.rcp[:, hh:hh + 1]))
                        assert ev == norm_pred[hh], (ev, norm_pred[hh])
                for pj in range(4):
                    if pj >= 1:
                        B.w(v, "vector", av_pred[pj - 1])
                        B.Vn.tensor_copy(
                            B.odm(pj - 1)[0:64, qc * 128:(qc + 1) * 128],
                            B.psB[0:64, 1024:1152])
                        ev = B.ev(B.sV, B.Vn.tensor_copy(
                            B.odm(pj - 1)[64:128, qc * 128:(qc + 1) * 128],
                            B.psB[64:128, 1536:1664]))
                        assert ev == oev_pred[pj - 1], (ev, oev_pred[pj - 1])
                    hh = 2 * pj
                    B.w(v, "vector", tp_pred[hh])
                    if hh == 0 and pv["oev"][3] is not None:
                        B.w(v, "vector", pv["oev"][3])
                    ev = B.ev(B.sV, B.Vn.tensor_copy(
                        B.attnT[0][:, 0:w],
                        B.psB[:, 0:w // 2].bitcast(BF16)))
                    assert ev == tev_pred[hh], (ev, tev_pred[hh])
                    B.w(v, "vector", tp_pred[hh + 1])
                    ev = B.ev(B.sV, v.tensor_copy(
                        B.attnT[1][:, 0:w],
                        B.psB[:, 512:512 + w // 2].bitcast(BF16)))
                    assert ev == tev_pred[hh + 1], (ev, tev_pred[hh + 1])
                B.w(v, "vector", av_pred[3])
                B.Vn.tensor_copy(B.odm(3)[0:64, qc * 128:(qc + 1) * 128],
                                 B.psB[0:64, 1024:1152])
                ev = B.ev(B.sV, v.tensor_copy(
                    B.odm(3)[64:128, qc * 128:(qc + 1) * 128],
                    B.psB[64:128, 1536:1664]))
                assert ev == oev_pred[3], (ev, oev_pred[3])

            @blk.scalar
            def _(sc, qc=qc, w=w, sim_pred=sim_pred, exp_pred=exp_pred,
                  tev_pred=tev_pred, tp_pred=tp_pred, oev_pred=oev_pred,
                  pv=pv, simreg=simreg):
                sc = B.S
                for hh in range(NH):
                    B.w(sc, "scalar", sim_pred[hh])
                    reg = simreg(hh)
                    ev = B.ev(B.sS, sc.activation(
                        B.probs[:, hh * 640:hh * 640 + w], reg[:, 0:w],
                        AF.Exp, accum_out=B.rsum[:, hh:hh + 1]))
                    assert ev == exp_pred[hh], (ev, exp_pred[hh])

            prev = {"exp": exp_pred, "tev": tev_pred, "oev": oev_pred}
        B.oev_all = list(prev["oev"])





def phase_wo(B, h, l, blk_=None):
    nc = B.nc
    with _maybe_block(nc, blk_) as blk:
        grp_of = {}
        ld_of = {}
        mm_ev = [None] * 8
        res_ev = [None] * 8

        def issue_wo_group(g, grp):
            c0 = grp * 4
            gi, par = B.ws.issue(g, [(
                B.wsb(B.ws.idx % 2)[:, 0:2048],
                bass.AP(B.d_wo.tensor, l * INNER * D + c0 * 128,
                        [[D, 128], [128 * D, 4], [1, 512]]))])
            for c in range(4):
                grp_of[c0 + c] = (gi, par)
                ld_of[c0 + c] = B.ws.load_ev[gi]

        @blk.gpsimd
        def _(g):

            g = B.G
            issue_wo_group(g, 0)
            issue_wo_group(g, 1)

        B.res_ev = res_ev
        for m in range(8):
            @blk.tensor
            def _(t, m=m):
                B.w(t, "tensor", ld_of[m], *B.oev_all)
                if m >= 2:
                    B.w(t, "tensor", res_ev[m - 2])
                gi, par = grp_of[m]
                reg = B.psA[:, 0:1024] if m % 2 == 0 else B.psA[:, 1024:2048]
                for kt in range(4):
                    off = kt * 512 + (m % 4) * 128
                    for tc in range(2):
                        ins = t.matmul(reg[:, tc * 512:(tc + 1) * 512],
                                       B.wsb(par)[:, off:off + 128],
                                       B.odm(kt)[:, tc * 512:(tc + 1) * 512],
                                       start=(kt == 0), stop=(kt == 3))
                mm_ev[m] = B.ev(B.sT, ins)
                if m == 3:
                    assert grp_of[0][1] == 0
                    B.ws.done(grp_of[0][0], mm_ev[m])
                if m == 7:
                    B.ws.done(grp_of[4][0], mm_ev[m])
                    B.wo_done = mm_ev

            @blk.vector
            def _(v, m=m):

                v = B.V
                B.w(v, "vector", mm_ev[m])
                reg = B.psA[:, 0:1024] if m % 2 == 0 else B.psA[:, 1024:2048]
                res_ev[m] = B.ev(B.sV, v.tensor_add(
                    B.u[m][:, 1:H + 1], reg, B.u[m][:, 1:H + 1]))

        if h == 0:
            @blk.vector
            def _(v):

                v = B.V
                B.w(v, "vector", *res_ev)
                for g in range(NG):
                    ins = v.tensor_copy(B.hL[l][:, g:g + 1], B.u[g][:, H:H + 1])
                B.ev(B.sV, ins)


def phase_ln(B, h, l, blk_=None):
    nc = B.nc
    SQB = [4096, 4608]
    W1B = 0

    with _maybe_block(nc, blk_) as blk:
        @blk.sync
        def _(s):
            if h == 0:
                B.w(s, "sync", (B.sPC[l], B.sPC[l].n))
            pref = {}
            for j in range(4):
                pref[j] = ffn_w1_load(B, s, l, j,
                                      [B.wo_done[3 if j < 2 else 7]])
            B.ffn_pref = pref

        sq_ev = [None] * NG
        lnmm_ev = [None] * NG
        nrm_ev = [None] * NG
        B.stgx_ev = [None] * NG

        @blk.vector
        def _(v):

            v = B.V
            if h == 1:
                for g in range(NG):
                    ins = v.tensor_copy(B.u[g][:, 0:1], B.hL[l][:, g:g + 1])
                B.ev(B.sV, ins)
            else:
                for g in range(NG):
                    ins = v.memset(B.u[g][:, 0:1], 0.0)
                B.ev(B.sV, ins)
            stg_ev = []
            for g in range(NG):
                off = 1 if g < 4 else 0
                B.w(v, "vector", B.res_ev[g])
                stg_ev.append(B.ev(B.sV, v.tensor_copy(
                    B.stg[g][:], B.u[g][:, off:off + H])))
            B.stg_ev = stg_ev

        for g in range(NG):
            @blk.vector
            def _(v, g=g):

                v = B.V
                B.w(v, "vector", B.stg_ev[g])
                if g >= 2:
                    B.w(v, "vector", lnmm_ev[g - 2])
                sq_ev[g] = B.ev(B.sV, v.tensor_mul(
                    B.unb(SQB[g % 2], 1024), B.stg[g][:], B.stg[g][:]))

            @blk.tensor
            def _(t, g=g):
                B.w(t, "tensor", sq_ev[g])
                if g == 0:
                    B.w(t, "tensor", B.res_ev[6], B.res_ev[7])
                srow = B.psA[0:1, 0:1024]
                qrow = B.psA[0:1, 1024:2048]
                for tc in range(2):
                    t.matmul(srow[:, tc * 512:(tc + 1) * 512], B.onesb[:],
                             B.stg[g][:, tc * 512:(tc + 1) * 512],
                             start=(g == 0), stop=(g == NG - 1))
                    ins = t.matmul(qrow[:, tc * 512:(tc + 1) * 512], B.onesb[:],
                                   B.unb(SQB[g % 2], 1024)[:, tc * 512:(tc + 1) * 512],
                                   start=(g == 0), stop=(g == NG - 1))
                lnmm_ev[g] = B.ev(B.sT, ins)

        @blk.vector
        def _(v):

            v = B.V
            B.w(v, "vector", lnmm_ev[NG - 1])
            srow = B.psA[0:1, 0:1024]
            qrow = B.psA[0:1, 1024:2048]
            v.tensor_scalar_mul(srow, srow, 1.0 / D)
            v.tensor_scalar_mul(qrow, qrow, 1.0 / D)
            v.tensor_copy(B.rowsSB[:, 0:1024], srow)
            v.tensor_tensor(B.rowsSB[:, 1024:2048], B.rowsSB[:, 0:1024],
                            B.rowsSB[:, 0:1024], ALU.mult)
            B.var_ev = B.ev(B.sV, v.tensor_sub(qrow, qrow,
                                               B.rowsSB[:, 1024:2048]))

        @blk.scalar
        def _(sc):

            sc = B.S
            B.w(sc, "scalar", B.var_ev)
            qrow = B.psA[0:1, 1024:2048]
            e = B.ev(B.sS, sc.activation(qrow, qrow, AF.Ln, bias=1e-5))
            B.w(sc, "scalar", e)
            B.rstd_ev = B.ev(B.sS, sc.activation(
                B.rowsSB[:, 0:1024], qrow, AF.Exp, scale=-0.5))

        @blk.vector
        def _(v):

            v = B.V
            B.w(v, "vector", B.rstd_ev)
            srow = B.psA[0:1, 0:1024]
            B.brow_ev = B.ev(B.sV, v.scalar_tensor_tensor(
                B.rowsSB[:, 1024:2048], srow, -1.0, B.rowsSB[:, 0:1024],
                ALU.mult, ALU.mult))

        @blk.tensor
        def _(t):
            B.w(t, "tensor", B.brow_ev)
            for tc in range(2):
                t.matmul(B.psA[:, tc * 512:(tc + 1) * 512], B.onesr[:],
                         B.rowsSB[:, tc * 512:(tc + 1) * 512],
                         start=True, stop=True)
                ins = t.matmul(B.psA[:, 1024 + tc * 512:1024 + (tc + 1) * 512],
                               B.onesr[:],
                               B.rowsSB[:, 1024 + tc * 512:1024 + (tc + 1) * 512],
                               start=True, stop=True)
            B.lnbc_ev = B.ev(B.sT, ins)

        stgx_pred = [(B.sS, B.sS.n + g + 1) for g in range(NG)]

        @blk.vector
        def _(v):
            v = B.V
            B.w(v, "vector", B.lnbc_ev)
            for g in range(NG):
                off = 1 if g < 4 else 0
                hsl = B.u[g][:, off:off + H]
                w1 = B.un(W1B + (g % 2) * 1024, 1024)
                if g >= 2:
                    B.wf(v, "vector", stgx_pred[g - 2])
                v.tensor_mul(w1, hsl, B.psA[:, 0:1024])
                nrm_ev[g] = B.ev(B.sV, v.tensor_add(
                    w1, w1, B.psA[:, 1024:2048]))

        @blk.scalar
        def _(sc):
            sc = B.S
            for g in range(NG):
                B.w(sc, "scalar", nrm_ev[g])
                ss = B.Sn if g < NG - 1 else B.S
                ev = B.ev(B.sS, ss.activation(
                    B.stg[g][:], B.un(W1B + (g % 2) * 1024, 1024), AF.Identity,
                    bias=B.lbc[:, g:g + 1], scale=B.lgc[:, g:g + 1]))
                assert ev == stgx_pred[g], (ev, stgx_pred[g])
                B.stgx_ev[g] = ev


def h2view(B, j):
    if j < 16:
        return B.bview(j // 2)[:, (j % 2) * H:(j % 2 + 1) * H]
    return B.unb((j - 16) * 512, 1024)


def ffn_w1_load(B, s, l, j, avail):
    o, ln = HCH[j]
    u = j % 4
    sem = B.sFu[u]
    B.wf(s, "sync", *avail)
    woff = B.pre_off[(l, "w1", j)]
    B.ev(sem, s.dma_start(
        out=B.wsu(u)[:, 0:8 * ln],
        in_=bass.AP(B.d_wpre.tensor, woff, [[8 * ln, 128], [1, 8 * ln]])), 16)
    B.ev(sem, s.dma_start(
        out=B.wsu(u)[:, 1024:1024 + 8 * ln],
        in_=bass.AP(B.d_wpre.tensor, woff + 1024 * ln,
                    [[8 * ln, 128], [1, 8 * ln]])), 16)
    return B.mark(sem)


def ffn_w2_load(B, s, l, m, avail):
    u0 = 2 if m % 2 == 0 else 0
    sem = B.sFu[u0]
    B.wf(s, "sync", *avail)
    base = B.pre_off[(l, "w2", m)]
    for c0, c1 in ((0, 896), (896, 1792), (1792, 2688)):
        B.ev(sem, s.dma_start(
            out=B.wsu2(u0)[:, c0:c1],
            in_=bass.AP(B.d_wpre.tensor, base + c0,
                        [[2688, 128], [1, c1 - c0]])), 16)
    B.ev(sem, s.dma_start(
        out=B.wsu2(u0)[0:42, 2688:2816],
        in_=bass.AP(B.d_wpre.tensor, B.pre_off[(l, "w2t", m)],
                    [[128, 42], [1, 128]])), 16)
    return B.mark(sem)


def phase_ffn(B, h, l, blk_=None):
    nc = B.nc
    GEL = [3072, 3584]

    with _maybe_block(nc, blk_) as blk:
        ld_of = {}
        w2ld = {}

        tbase, sbase, vbase = B.sT.n, B.sS.n, B.sV.n
        mm_pred = [(B.sT, tbase + j + 1) for j in range(NHCH)]
        w2mm_pred = [(B.sT, tbase + NHCH + m + 1) for m in range(8)]
        B.w2mm_pred = w2mm_pred
        B.ffn_done_evs = None
        gel_pred = [(B.sS, sbase + j + 1) for j in range(NHCH)]
        gate_pred = [(B.sV, vbase + j + 1) for j in range(NHCH)]
        w2res_pred = [(B.sV, vbase + NHCH + m + 1) for m in range(8)]
        B.ffn_done_evs = (w2res_pred[7], (B.sS, sbase + NHCH))

        @blk.sync
        def _(s):
            for j in range(4):
                ld_of[j] = B.ffn_pref[j]
            for j in range(4, NHCH):
                ld_of[j] = ffn_w1_load(B, s, l, j, [mm_pred[j - 4]])
            for m in range(8):
                if m == 0:
                    avail = [mm_pred[18], mm_pred[19]]
                elif m == 1:
                    avail = [mm_pred[20], mm_pred[21]]
                else:
                    avail = [w2mm_pred[m - 2]]
                w2ld[m] = ffn_w2_load(B, s, l, m, avail)

        @blk.tensor
        def _(t):
            for j in range(NHCH):
                o, ln = HCH[j]
                B.w(t, "tensor", ld_of[j])
                if j == 0:
                    pass
                elif j == 1:
                    B.w(t, "tensor", B.stgx_ev[7])
                else:
                    B.w(t, "tensor", *B.stgx_ev)
                if j >= 2:
                    B.wf(t, "tensor", gate_pred[j - 2])
                wu = B.wsu(j % 4)
                rA = B.psB[:, 0:1024] if j % 2 == 0 else B.psA[:, 0:1024]
                rG = B.psB[:, 1024:2048] if j % 2 == 0 else B.psA[:, 1024:2048]
                for kt in range(NG):
                    if j == 0:
                        B.w(t, "tensor", B.stgx_ev[kt])
                    for tc in range(2):
                        t.matmul(rA[0:ln, tc * 512:(tc + 1) * 512],
                                 wu[:, kt * ln:(kt + 1) * ln],
                                 B.stg[kt][:, tc * 512:(tc + 1) * 512],
                                 start=(kt == 0), stop=(kt == NG - 1))
                        ins = t.matmul(
                            rG[0:ln, tc * 512:(tc + 1) * 512],
                            wu[:, 1024 + kt * ln:1024 + (kt + 1) * ln],
                            B.stg[kt][:, tc * 512:(tc + 1) * 512],
                            start=(kt == 0), stop=(kt == NG - 1))
                ev = B.ev(B.sT, ins)
                assert ev == mm_pred[j], (ev, mm_pred[j])
            for m in range(8):
                B.w(t, "tensor", w2ld[m])
                B.wf(t, "tensor", gate_pred[NHCH - 1])
                if m >= 2:
                    B.wf(t, "tensor", w2res_pred[m - 2])
                wu2 = B.wsu2(2 if m % 2 == 0 else 0)
                reg = B.psA[:, 0:1024] if m % 2 == 0 else B.psA[:, 1024:2048]
                for j in range(NHCH):
                    o, ln = HCH[j]
                    for tc in range(2):
                        ins = t.matmul(reg[:, tc * 512:(tc + 1) * 512],
                                       wu2[0:ln, j * 128:j * 128 + 128],
                                       h2view(B, j)[0:ln, tc * 512:(tc + 1) * 512],
                                       start=(j == 0), stop=(j == NHCH - 1))
                ev = B.ev(B.sT, ins)
                assert ev == w2mm_pred[m], (ev, w2mm_pred[m])

        @blk.scalar
        def _(sc):
            sc = B.S
            for j in range(NHCH):
                o, ln = HCH[j]
                B.w(sc, "scalar", mm_pred[j])
                if j >= 2:
                    B.wf(sc, "scalar", gate_pred[j - 2])
                rG = B.psB[:, 1024:2048] if j % 2 == 0 else B.psA[:, 1024:2048]
                ss = B.Sn if j < NHCH - 1 else B.S
                ev = B.ev(B.sS, ss.activation(
                    B.unb(GEL[j % 2], 1024)[0:ln, :], rG[0:ln, :], AF.Gelu,
                    bias=B.b1g[0:ln, j:j + 1]))
                assert ev == gel_pred[j], (ev, gel_pred[j])

        @blk.vector
        def _(v):
            v = B.V
            for j in range(NHCH):
                o, ln = HCH[j]
                B.w(v, "vector", gel_pred[j])
                rA = B.psB[:, 0:1024] if j % 2 == 0 else B.psA[:, 0:1024]
                vv = B.Vn if j < NHCH - 1 else B.V
                ev = B.ev(B.sV, vv.scalar_tensor_tensor(
                    h2view(B, j)[0:ln, :], rA[0:ln, :], B.b1a[0:ln, j:j + 1],
                    B.unb(GEL[j % 2], 1024)[0:ln, :], ALU.add, ALU.mult))
                assert ev == gate_pred[j], (ev, gate_pred[j])
            for m in range(8):
                B.w(v, "vector", w2mm_pred[m])
                reg = B.psA[:, 0:1024] if m % 2 == 0 else B.psA[:, 1024:2048]
                ev = B.ev(B.sV, v.scalar_tensor_tensor(
                    B.u[m][:, 1:H + 1], reg, B.b2c[:, m:m + 1],
                    B.u[m][:, 1:H + 1], ALU.add, ALU.add))
                assert ev == w2res_pred[m], (ev, w2res_pred[m])



def phase_final(B, h, blk_=None):
    nc = B.nc
    t0 = h * H
    XTI = [4096, 5120]

    with _maybe_block(nc, blk_) as blk:
        sum_ev = [None] * NG
        tp_ev = [None] * 8
        add_ev = [None] * 8
        x_ev = [None] * 8
        st_ev = [None] * 8

        for g in range(NG):
            @blk.vector
            def _(v, g=g):

                v = B.V
                B.wf(v, "vector", B.w2mm_pred[7])
                sum_ev[g] = B.ev(B.sV, v.tensor_add(
                    B.Bb[g][:, 0:H], B.parkSB[g][:], B.u[g][:, 1:H + 1]))

        def issue_xf(s, tb):
            if tb >= 2:
                B.w(s, "sync", st_ev[tb - 2])
            sem = B.sXp[tb % 2]
            B.ev(sem, s.dma_start(
                out=B.un(XTI[tb % 2], 1024),
                in_=B.d_x[t0 + tb * 128:t0 + (tb + 1) * 128, :]), 16)
            x_ev[tb] = B.mark(sem)

        @blk.sync
        def _(s):
            issue_xf(s, 0)
            issue_xf(s, 1)

        for tb in range(8):
            @blk.tensor
            def _(t, tb=tb):
                B.w(t, "tensor", *sum_ev)
                if tb >= 2:
                    B.w(t, "tensor", add_ev[tb - 2])
                reg = B.psb(0 if tb % 2 == 0 else 4, 1024)
                for g in range(NG):
                    ins = t.transpose(reg[:, g * 128:(g + 1) * 128],
                                      B.Bb[g][:, tb * 128:(tb + 1) * 128],
                                      B.ident32[:])
                tp_ev[tb] = B.ev(B.sT, ins)

            @blk.vector
            def _(v, tb=tb):

                v = B.V
                B.w(v, "vector", tp_ev[tb], x_ev[tb])
                if tb >= 2:
                    B.w(v, "vector", st_ev[tb - 2])
                reg = B.psb(0 if tb % 2 == 0 else 4, 1024)
                add_ev[tb] = B.ev(B.sV, v.tensor_add(
                    B.un(XTI[tb % 2], 1024), B.un(XTI[tb % 2], 1024), reg))

            @blk.sync
            def _(s, tb=tb):
                B.w(s, "sync", add_ev[tb])
                sem = B.sPp[tb % 2]
                B.ev(sem, s.dma_start(
                    out=B.d_out[t0 + tb * 128:t0 + (tb + 1) * 128, :],
                    in_=B.un(XTI[tb % 2], 1024)), 16)
                st_ev[tb] = B.mark(sem)
                if tb + 2 < 8:
                    issue_xf(s, tb + 2)
                if tb == 7:
                    B.final_done = (B.mark(B.sPp[0]), B.mark(B.sPp[1]))


def phase_dbg_qkv(B, h):
    t0 = h * H
    with B.nc.Block() as blk:
        @blk.gpsimd
        def _(g):
            for gg in range(4):
                B.ev(B.sXG, g.dma_start(
                    out=B.d_dbg["dbg_q"][gg * 128:(gg + 1) * 128, t0:t0 + H],
                    in_=B.qp(gg)), 16)
                B.ev(B.sXG, g.dma_start(
                    out=B.d_dbg["dbg_k"][gg * 128:(gg + 1) * 128, t0:t0 + H],
                    in_=B.kp(gg)), 16)
            for tb in range(8):
                B.ev(B.sXG, g.dma_start(
                    out=B.d_dbg["dbg_v"][(h * 8 + tb) * 128:(h * 8 + tb + 1) * 128, :],
                    in_=B.vtm(tb)), 16)
            g.wait_ge(B.sXG.h, B.sXG.n)


def phase_dbg_o(B, h):
    t0 = h * H
    with B.nc.Block() as blk:
        @blk.gpsimd
        def _(g):
            for gg in range(4):
                B.ev(B.sXG, g.dma_start(
                    out=B.d_dbg["dbg_o"][gg * 128:(gg + 1) * 128, t0:t0 + H],
                    in_=B.odm(gg)), 16)
            g.wait_ge(B.sXG.h, B.sXG.n)


def phase_dbg_u(B, h, name):
    nc = B.nc
    t0 = h * H
    with nc.Block() as blk:
        @blk.sync
        def _(s):
            for g in range(NG):
                B.ev(B.sP, s.dma_start(
                    out=B.d_dbg[name][g * 128:(g + 1) * 128, t0:t0 + H],
                    in_=B.u[g][:, 1:H + 1]), 16)
            B.w(s, "sync", B.mark(B.sP))


_BUILT = {}


def get_built(debug=False):
    if debug not in _BUILT:
        _BUILT[debug] = build(debug=debug)
    return _BUILT[debug]


def make_in_maps(B, inputs):
    x = np.ascontiguousarray(np.asarray(inputs["L0_attn_x"], dtype=np.float32))
    shared = {
        "ssm_a": inputs["L0_attn_a"], "ssm_b": inputs["L0_attn_b"],
        "b_out": inputs["L0_attn_b_out"],
        "wqkv": inputs["L0_ffn_wqkv"], "q_scale": inputs["L0_ffn_q_scale"],
        "k_scale": inputs["L0_ffn_k_scale"], "wo": inputs["L0_ffn_wo"],
        "ln_g": inputs["L0_ffn_ln_g"], "ln_b": inputs["L0_ffn_ln_b"],
        "b1": inputs["L0_ffn_b1"], "b2": inputs["L0_ffn_b2"],
    }
    shared = {k: np.ascontiguousarray(np.asarray(v, np.float32))
              for k, v in shared.items()}
    wout = np.asarray(inputs["L0_attn_w_out"], np.float32)
    shared["woutp"] = np.ascontiguousarray(np.concatenate(
        [wout[ec * 128:(ec + 1) * 128, :].reshape(128, 8, 128)
         .transpose(2, 1, 0).ravel() for ec in range(8)]))
    w1 = np.asarray(inputs["L0_ffn_w1"], np.float32)
    w2 = np.asarray(inputs["L0_ffn_w2"], np.float32)
    blocks = []
    for l in range(NL):
        for j in range(NHCH):
            o, ln = HCH[j]
            for b in range(2):
                blocks.append(w1[l][:, b * HID + o:b * HID + o + ln]
                              .reshape(8, 128, ln).transpose(1, 0, 2).ravel())
    shared["w1p"] = np.ascontiguousarray(np.concatenate(blocks))
    blocks = []
    for l in range(NL):
        for m in range(8):
            blocks.append(w2[l][:21 * 128, m * 128:(m + 1) * 128]
                          .reshape(21, 128, 128).transpose(1, 0, 2).ravel())
            blocks.append(w2[l][21 * 128:, m * 128:(m + 1) * 128].ravel())
    shared["w2p"] = np.ascontiguousarray(np.concatenate(blocks))
    shared.update({k: np.ascontiguousarray(v) for k, v in B.consts.items()})
    in_maps = []
    for c in range(8):
        m = dict(shared)
        m["x"] = np.ascontiguousarray(x[c])
        in_maps.append(m)
    return in_maps


def kernel(**inputs):
    B = get_built(debug=False)
    in_maps = make_in_maps(B, inputs)
    res = run_bass_kernel_spmd(B.nc, in_maps, core_ids=list(range(8)))
    return np.stack([r["out"] for r in res.results], axis=0)
